# version 04
# baseline (speedup 1.0000x reference)
import math
import numpy as np
import ml_dtypes
from contextlib import ExitStack

import concourse.bass as bass
import concourse.mybir as mybir
from concourse.bass_utils import run_bass_kernel_spmd

F32 = mybir.dt.float32
BF16 = mybir.dt.bfloat16
F16 = mybir.dt.float16
I16 = mybir.dt.int16
AF = mybir.ActivationFunctionType
ALU = mybir.AluOpType
AXX = mybir.AxisListType.X

B, T, C, H, Dh = 8, 2048, 1024, 16, 64
C2 = 2048


def get_slopes(n):
    def pow2(n):
        start = 2 ** (-2 ** (-(math.log2(n) - 3)))
        return [start * start ** i for i in range(n)]
    if math.log2(n).is_integer():
        return pow2(n)
    c = 2 ** math.floor(math.log2(n))
    return pow2(c) + get_slopes(2 * c)[0::2][:n - c]

SLOPES = get_slopes(H)

ALL_STAGES = ("attn_a", "ffn1", "attn_b", "ffn2")


class Entry:
    __slots__ = ("thunk", "waits", "inc", "inc_n", "dma", "mode", "prewait",
                 "planned_val")

    def __init__(self, thunk, waits, inc, inc_n, dma):
        self.thunk, self.waits = thunk, waits
        self.inc, self.inc_n, self.dma = inc, inc_n, dma
        self.mode = "op"
        self.prewait = None


class Plan:
    ENGS = ("t", "v", "s", "g", "y")
    ESEM = {"t": "T", "v": "V", "s": "S", "g": "G", "y": "Y"}

    def __init__(self):
        self.q = {e: [] for e in self.ENGS}
        self.cnt = {k: 0 for k in ("T", "V", "S", "G", "Y", "DG", "DS", "DO")}
        self.waited = {e: {} for e in self.ENGS}
        self.guards = {}

    def emit(self, eng, thunk, wait=(), inc=None, inc_n=1, dma=False):
        waits = []
        for k, val in wait:
            if val <= 0:
                continue
            if val > self.waited[eng].get(k, -1):
                waits.append((k, val))
                self.waited[eng][k] = val
        val_out = None
        if inc is not None:
            self.cnt[inc] += inc_n
            val_out = self.cnt[inc]
        ent = Entry(thunk, waits, inc, inc_n, dma)
        ent.planned_val = val_out
        self.q[eng].append(ent)
        return val_out

    def t(self, thunk, wait=(), inc=None):
        return self.emit("t", thunk, wait, inc)

    def v(self, thunk, wait=(), inc=None):
        return self.emit("v", thunk, wait, inc)

    def s(self, thunk, wait=(), inc=None):
        return self.emit("s", thunk, wait, inc)

    def g(self, thunk, wait=(), inc=None, inc_n=1, dma=False):
        return self.emit("g", thunk, wait, inc, inc_n, dma)

    def y(self, thunk, wait=(), inc=None, inc_n=1, dma=False):
        return self.emit("y", thunk, wait, inc, inc_n, dma)

    def consume(self, bank, semkey, val):
        self.guards.setdefault(bank, {})
        self.guards[bank][semkey] = max(self.guards[bank].get(semkey, 0), val)

    def twrite(self, bank):
        g = self.guards.pop(bank, {})
        return list(g.items())

    def barrier(self):
        vals = {}
        for eng in self.ENGS:
            k = self.ESEM[eng]
            if eng == "y":
                self.emit(eng, None,
                          [("DS", self.cnt["DS"]), ("DO", self.cnt["DO"])])
            vals[k] = self.emit(eng, None, (), k)
        for eng in self.ENGS:
            w = [(k, v) for k, v in vals.items() if k != self.ESEM[eng]]
            self.emit(eng, None, w)
        self.guards.clear()

    def finalize(self):
        for eng in self.ENGS:
            carrier = None
            for ent in self.q[eng]:
                if ent.thunk is not None:
                    if not ent.dma:
                        carrier = ent
                    continue
                if ent.inc is None:
                    ent.mode = "waits"
                    continue
                if carrier is not None and carrier.inc is None and not ent.waits:
                    carrier.inc, carrier.inc_n = ent.inc, ent.inc_n
                    carrier.planned_val = ent.planned_val
                    ent.mode = "waits"
                elif carrier is not None and carrier.inc is not None:
                    ent.mode = "seminc"
                    ent.prewait = (carrier.inc, self._planned_val(carrier))
                else:
                    ent.mode = "seminc"

    def _planned_val(self, ent):
        return ent.planned_val


def build_program(stages=ALL_STAGES, dumps=()):
    nc = bass.Bass("TRN2", target_bir_lowering=False)

    d_xT = nc.dram_tensor("xT", [C, T], F32, kind="ExternalInput").ap()
    d_wqkv = nc.dram_tensor("wqkv", [4, 128, 6144], F32, kind="ExternalInput").ap()
    d_wqkvb = nc.dram_tensor("wqkvb", [4, 128, 6144], F32, kind="ExternalInput").ap()
    d_wproj = nc.dram_tensor("wproj", [2, 128, 4096], F32, kind="ExternalInput").ap()
    d_wc = nc.dram_tensor("wc", [128, 8192], F32, kind="ExternalInput").ap()
    d_wr = nc.dram_tensor("wr", [128, 8192], F32, kind="ExternalInput").ap()
    d_wk2 = nc.dram_tensor("wk2", [128, 8192], F32, kind="ExternalInput").ap()
    d_wv2 = nc.dram_tensor("wv2", [128, 8192], F32, kind="ExternalInput").ap()
    d_ltriT = nc.dram_tensor("ltriT", [64, 1024], F32, kind="ExternalInput").ap()
    d_vecs = nc.dram_tensor("vecs", [128, 48], F32, kind="ExternalInput").ap()
    d_ident = nc.dram_tensor("ident", [128, 128], F32, kind="ExternalInput").ap()
    d_triu = nc.dram_tensor("triu", [128, 128], F32, kind="ExternalInput").ap()
    d_iotaf = nc.dram_tensor("iotaf", [128, 2048], F16, kind="ExternalInput").ap()
    d_bvrow = nc.dram_tensor("bvrow", [1, 1024], F32, kind="ExternalInput").ap()
    d_out = nc.dram_tensor("out", [C, T], F32, kind="ExternalOutput").ap()
    d_dump = {}
    for nm in dumps:
        d_dump[nm] = nc.dram_tensor("dump_" + nm, [128, 16384], F32,
                                    kind="ExternalOutput").ap()

    pl = Plan()

    with ExitStack() as ctx:
        SBT = lambda n, sh, dt: ctx.enter_context(nc.sbuf_tensor(n, sh, dt))
        PST = lambda n, sh, dt: ctx.enter_context(nc.psum_tensor(n, sh, dt))

        X = SBT("X", [128, 16384], F32)
        LNB = SBT("LNB", [128, 16384], BF16)
        ARN = SBT("ARN", [128, 16384], BF16)
        W48 = SBT("W48", [128, 24576], BF16)
        WQS = SBT("WQS", [128, 6144], BF16)
        BCS = SBT("BCS", [128, 1024], F32)
        RWB = SBT("RWB", [128, 2048], F32)
        VEC = SBT("VEC", [128, 64], F32)
        KTOK = SBT("KTOK", [128, 256], BF16)
        KVSB = SBT("KVSB", [128, 384], BF16)
        PTB = SBT("PTB", [128, 1024], BF16)
        E0B = SBT("E0B", [128, 128], BF16)
        EEB = SBT("EEB", [128, 128], BF16)
        RB16 = SBT("RB16", [128, 1024], BF16)
        TRIU = SBT("TRIU", [128, 128], F32)
        IDF = SBT("IDF", [128, 128], F16)
        IDB = SBT("IDB", [128, 128], BF16)
        ONB = SBT("ONB", [128, 8], BF16)
        ONB0 = SBT("ONB0", [128, 128], BF16)
        ONB32 = SBT("ONB32", [128, 128], BF16)

        MST = W48.ap()[:, 14336:18432].bitcast(F16)
        IOTA = W48.ap()[:, 18432:20480].bitcast(F16)
        PTL = W48.ap()[:, 20480:24576]

        SPP = [PST("SPP0", [128, 1024], F32), PST("SPP1", [128, 1024], F32)]
        PS47 = [PST(f"PS{i}", [128, 512], F32) for i in range(3)]
        PS = [SPP[0].ap()[:, 0:512], SPP[1].ap()[:, 0:512],
              SPP[0].ap()[:, 512:1024], SPP[1].ap()[:, 512:1024],
              PS47[0].ap()[:, :], PS47[1].ap()[:, :], PS47[2].ap()[:, :]]
        PSB = PST("PSB", [128, 1024], BF16)

        sem = {k: ctx.enter_context(nc.semaphore("sem" + k))
               for k in ("T", "V", "S", "G", "Y", "DG", "DS", "DO")}

        def xc(c, q0=0, w=C2):
            return X[:, C2 * c + q0: C2 * c + q0 + w]

        def lnc(c, q0=0, w=C2):
            return LNB[:, C2 * c + q0: C2 * c + q0 + w]

        def load_w(dst_ap, src_ap, wait=()):
            return pl.g(lambda e: e.dma_start(out=dst_ap, in_=src_ap),
                        wait=wait, inc="DG", inc_n=16, dma=True)

        def bcast(bank_ap, bname, src_ap, sel, nrows, w, wait=()):
            hi = RB16[0:nrows, 0:w]
            lo = RB16[0:nrows, 512:512 + w]
            v1 = pl.v(lambda e: e.tensor_copy(hi, src_ap),
                      wait=list(wait) + pl.twrite("rb16"), inc="V")
            v2 = pl.v(lambda e: e.tensor_tensor(lo, src_ap, hi, ALU.subtract),
                      wait=[("V", v1)], inc="V")
            wg = pl.twrite(bname)
            pl.t(lambda e: e.matmul(bank_ap, sel, RB16[:, 0:w],
                                    start=True, stop=False),
                 wait=[("V", v2)] + wg)
            tb = pl.t(lambda e: e.matmul(bank_ap, sel, RB16[:, 512:512 + w],
                                         start=False, stop=True), inc="T")
            pl.consume("rb16", "T", tb)
            return tb

        for c in range(8):
            pl.y(lambda e, c=c: e.dma_start(out=X[:, C2*c:C2*(c+1)],
                                            in_=d_xT[128*c:128*(c+1), :]),
                 inc="DS", inc_n=16, dma=True)
        pl.y(lambda e: e.dma_start(out=VEC[:, 0:48], in_=d_vecs), inc="DS", inc_n=16, dma=True)
        pl.y(lambda e: e.dma_start(out=RWB[0:1, 1024:2048], in_=d_bvrow),
             inc="DS", inc_n=16, dma=True)
        dma_init = pl.cnt["DS"]

        pl.g(lambda e: e.dma_start(out=IOTA, in_=d_iotaf), inc="DG", inc_n=16,
             dma=True)
        pl.g(lambda e: e.dma_start(out=IDF[:, :], in_=d_ident), inc="DG",
             inc_n=16, dma=True)
        pl.g(lambda e: e.dma_start(out=IDB[:, :], in_=d_ident), inc="DG",
             inc_n=16, dma=True)
        pl.y(lambda e: e.dma_start(out=TRIU[:, :], in_=d_triu), inc="DS",
             inc_n=16, dma=True)
        pl.g(lambda e: e.memset(ONB[:, :], 1.0))
        pl.g(lambda e: e.memset(ONB0[:, :], 0.0))
        pl.g(lambda e: e.memset(ONB0[:, 0:1], 1.0))
        pl.g(lambda e: e.memset(ONB32[:, :], 0.0))
        pl.g(lambda e: e.memset(ONB32[:, 32:33], 1.0))
        pl.g(lambda e: e.memset(W48[:, 8192:14336], 0.0))
        pl.g(lambda e: e.memset(E0B[:, :], 0.0))
        pl.g(lambda e: e.memset(E0B[0:1, :], 1.0))
        pl.g(lambda e: e.memset(EEB[:, :], 0.0))
        pl.g(lambda e: e.memset(EEB[0:1, 0:64], 1.0))
        pl.g(lambda e: e.memset(EEB[32:33, 64:128], 1.0))
        pl.g(lambda e: e.memset(RB16[:, :], 0.0))
        pl.g(lambda e: e.memset(RWB[:, 0:1024], 0.0))
        pl.g(lambda e: e.memset(KVSB[:, :], 0.0))
        pl.g(lambda e: e.memset(VEC[:, 57:58], 1e-5), inc="G")
        pl.s(lambda e: e.mul(VEC[:, 49:57], VEC[:, 0:8], 0.125),
             wait=[("DS", dma_init)], inc="S")
        load_w(WQS[:, :], d_wqkv[0])
        pl.barrier()

        import os as _os
        LN_CUT = int(_os.environ.get("LN_CUT", "9"))

        def ph_layernorm():
            bsum, bsq, binv, bnm = PS[0], PS[1], PS[2], PS[3]
            smm = {}
            for qt in range(4):
                q0 = 512 * qt
                tv = None
                for c in range(8):
                    gi = 8 * qt + c
                    xb = ARN[:, 8192 + 1024 * (c % 2): 8192 + 1024 * (c % 2) + 512]
                    sq = ARN[:, 12288 + 1024 * (c % 2): 12288 + 1024 * (c % 2) + 512]
                    wbuf = [("T", smm[gi - 2])] if gi - 2 in smm else []
                    vv = pl.v(lambda e, xb=xb, c=c, q0=q0:
                              e.tensor_copy(xb, xc(c, q0, 512)),
                              wait=wbuf, inc="V")
                    ss = pl.s(lambda e, sq=sq, c=c, q0=q0:
                              e.activation(sq, xc(c, q0, 512), AF.Square),
                              wait=wbuf, inc="S")
                    wg = (pl.twrite("b0") + pl.twrite("b1")) if c == 0 else []
                    pl.t(lambda e, xb=xb, c=c:
                         e.matmul(bsum[0:1, :], ONB[:, 0:1], xb,
                                  start=(c == 0), stop=(c == 7)),
                         wait=[("V", vv)] + wg)
                    tv = pl.t(lambda e, sq=sq, c=c:
                              e.matmul(bsq[0:1, :], ONB[:, 0:1], sq,
                                       start=(c == 0), stop=(c == 7)),
                              wait=[("S", ss)], inc="T")
                    smm[gi] = tv
                if LN_CUT < 2:
                    continue
                mu = RWB[0:1, 0:512]
                m2 = RWB[0:1, 512:1024]
                v1 = pl.v(lambda e: e.tensor_scalar(mu, bsum[0:1, :], 1.0 / 1024,
                                                    None, ALU.mult),
                          wait=[("T", tv)], inc="V")
                pl.consume("b0", "V", v1)
                v2 = pl.v(lambda e: e.tensor_tensor(m2, mu, mu, ALU.mult),
                          wait=[("V", v1)], inc="V")
                v3 = pl.v(lambda e: e.scalar_tensor_tensor(m2, bsq[0:1, :],
                                                           1.0 / 1024, m2,
                                                           ALU.mult, ALU.subtract),
                          wait=[("V", v2), ("T", tv)], inc="V")
                pl.consume("b1", "V", v3)
                if LN_CUT < 3:
                    continue
                s1 = pl.s(lambda e: e.activation(m2, m2, AF.Ln,
                                                 bias=VEC[0:1, 57:58]),
                          wait=[("V", v3)], inc="S")
                s1b = pl.s(lambda e: e.activation(m2, m2, AF.Exp, scale=-0.5),
                           wait=[("S", s1)], inc="S")
                v4 = s1b

                v5 = pl.v(lambda e: e.scalar_tensor_tensor(mu, mu, -1.0, m2,
                                                           ALU.mult, ALU.mult),
                          wait=[("S", v4)], inc="V")
                if LN_CUT < 4:
                    continue
                bcast(binv[:, :], "b2", RWB[0:1, 512:1024], E0B[:, :], 1, 512,
                      wait=[("S", v4)])
                tb = bcast(bnm[:, :], "b3", RWB[0:1, 0:512], E0B[:, :], 1, 512,
                           wait=[("V", v5)])
                if LN_CUT < 5:
                    continue
                vn = None
                for c in range(8):
                    tsb = BCS[:, 512 * (c % 2): 512 * (c % 2) + 512]
                    pl.v(lambda e, tsb=tsb, c=c, q0=q0:
                         e.tensor_tensor(tsb, xc(c, q0, 512), binv[:, :], ALU.mult),
                         wait=[("T", tb)], inc="V")
                    vn = pl.v(lambda e, tsb=tsb, c=c, q0=q0:
                              e.tensor_tensor(lnc(c, q0, 512), tsb, bnm[:, :],
                                              ALU.add), inc="V")
                pl.consume("b2", "V", vn)
                pl.consume("b3", "V", vn)

        ACUT = int(_os.environ.get("ATTN_CUT", "9"))

        def ph_attn_a():
            wq_dma = pl.cnt["DG"]
            for q in range(4):
                wq_ready = wq_dma
                last_qkv_t = None
                for i in range(2):
                    gqc = 2 * q + i
                    for sec, base, bias_col, scale in (
                            (0, 0, 49 + gqc, 0.125), (1, 4096, 8 + gqc, 1.0)):
                        for qt in range(4):
                            bank = PS[2 * sec + qt % 2]
                            bname = f"b{2 * sec + qt % 2}"
                            tvv = None
                            for c in range(8):
                                wg = pl.twrite(bname) if c == 0 else []
                                col = (c * 3 + sec) * 256 + 128 * i
                                tvv = pl.t(
                                    lambda e, bank=bank, c=c, col=col, q0=512*qt:
                                    e.matmul(bank[:, :], WQS[:, col:col+128],
                                             lnc(c, q0, 512),
                                             start=(c == 0), stop=(c == 7)),
                                    wait=[("DG", wq_ready)] + wg,
                                    inc="T" if c == 7 else None)
                            sv = pl.s(
                                lambda e, bank=bank, base=base, i=i, qt=qt,
                                       bc=bias_col, sc=scale:
                                e.activation(
                                    W48[:, base + 2048*i + 512*qt:
                                           base + 2048*i + 512*qt + 512],
                                    bank[:, :], AF.Identity,
                                    bias=VEC[:, bc:bc+1], scale=sc),
                                wait=[("T", tvv)], inc="S")
                            pl.consume(bname, "S", sv)
                            last_qkv_t = tvv
                tb = bcast(PS[6][:, 0:256], "b6",
                           RWB[0:1, 1024 + 256*q: 1024 + 256*q + 256],
                           E0B[:, :], 1, 256)
                sb = pl.s(lambda e: e.copy(BCS[:, 0:256], PS[6][:, 0:256]),
                          wait=[("T", tb)], inc="S")
                pl.consume("b6", "S", sb)
                for tt in range(16):
                    bank = PS[4 + tt % 2]
                    bname = f"b{4 + tt % 2}"
                    tvv = None
                    for c in range(8):
                        wg = pl.twrite(bname) if c == 0 else []
                        col = (c * 3 + 2) * 256
                        tvv = pl.t(
                            lambda e, bank=bank, c=c, col=col, tt=tt:
                            e.matmul(bank[:, 0:256], lnc(c, 128*tt, 128),
                                     WQS[:, col:col+256],
                                     start=(c == 0), stop=(c == 7)),
                            wait=[("DG", wq_ready)] + wg,
                            inc="T" if c == 7 else None)
                    vout = W48.ap()[:, 8192 + 384*tt: 8192 + 384*tt + 384] \
                        .rearrange("p (ip t3 r) -> p ip t3 r", ip=2, t3=3)[:, :, 0:3:2, :]
                    vv = pl.v(lambda e, bank=bank, vout=vout: e.tensor_tensor(
                        vout,
                        bank[:, 0:256].rearrange("p (ip hl r) -> p ip hl r", ip=2, hl=2),
                        BCS.ap()[:, 0:256].rearrange("p (ip hl r) -> p ip hl r",
                                                     ip=2, hl=2)[:, :, :, :],
                        ALU.add),
                        wait=[("T", tvv), ("S", sb)], inc="V")
                    pl.consume(bname, "V", vv)
                    last_qkv_t = tvv
                if q == 0:
                    dump_w48("w48q0")
                if q < 3:
                    wq_dma = load_w(WQS[:, :], d_wqkv[q + 1],
                                    wait=[("T", last_qkv_t)])
                else:
                    load_w(WQS[:, 0:4096], d_wproj[0],
                           wait=[("T", last_qkv_t)])
                pl.barrier()

                if ACUT < 2:
                    continue
                prev_stats_t = 0
                for mi, base in ((0, 0), (1, 4096)):
                    for i in range(2):
                        vv = pl.v(lambda e, base=base, i=i: e.tensor_tensor(
                            PTL[:, 0:2048],
                            W48[:, base + 2048*i: base + 2048*i + 2048],
                            W48[:, base + 2048*i: base + 2048*i + 2048], ALU.mult),
                            wait=[("T", prev_stats_t)], inc="V")
                        for qt in range(4):
                            wg = pl.twrite(f"b{qt}") if i == 0 else []
                            tvv = pl.t(lambda e, qt=qt, i=i: e.matmul(
                                PS[qt][0:1, :], ONB[:, 0:1],
                                PTL[:, 512*qt: 512*qt + 512],
                                start=(i == 0), stop=(i == 1)),
                                wait=[("V", vv)] + wg,
                                inc="T")
                            prev_stats_t = tvv
                            if i == 1:
                                vv2 = pl.v(lambda e, qt=qt, mi=mi: e.reduce_max(
                                    RWB[0:1, 4*mi + qt: 4*mi + qt + 1],
                                    PS[qt][0:1, :], axis=AXX),
                                    wait=[("T", tvv)], inc="V")
                                pl.consume(f"b{qt}", "V", vv2)
                v6 = pl.v(lambda e: e.reduce_max(RWB[0:1, 8:9], RWB[0:1, 0:4],
                                                 axis=AXX), inc="V")
                v7 = pl.v(lambda e: e.reduce_max(RWB[0:1, 9:10], RWB[0:1, 4:8],
                                                 axis=AXX), wait=[("V", v6)],
                          inc="V")
                v8 = pl.v(lambda e: e.tensor_tensor(RWB[0:1, 8:9], RWB[0:1, 8:9],
                                                    RWB[0:1, 9:10], ALU.mult),
                          wait=[("V", v7)], inc="V")
                s2 = pl.s(lambda e: e.activation(RWB[0:1, 8:9], RWB[0:1, 8:9],
                                                 AF.Sqrt),
                          wait=[("V", v8)], inc="S")
                v9 = pl.v(lambda e: e.tensor_scalar(RWB[0:1, 8:9], RWB[0:1, 8:9],
                                                    -1.0, None, ALU.mult),
                          wait=[("S", s2)], inc="V")
                tb2 = bcast(PS[6][:, 0:1], "b6", RWB[0:1, 8:9], E0B[:, :], 1, 1,
                            wait=[("V", v9)])
                sm = pl.s(lambda e, q=q: e.copy(VEC[:, 58 + q % 2: 59 + q % 2],
                                                PS[6][:, 0:1]),
                          wait=[("T", tb2)], inc="S")
                pl.consume("b6", "S", sm)

                if ACUT < 3:
                    continue
                for i in range(2):
                    mg = []
                    for hl in range(2):
                        slope = SLOPES[4 * q + 2 * i + hl]
                        mv = pl.v(lambda e, hl=hl, slope=slope: e.tensor_scalar(
                            MST[:, 2048*hl: 2048*hl + 2048], IOTA, 0.0, slope,
                            ALU.min, ALU.mult),
                            wait=[("S", sm), ("DG", 48)] +
                                 (pl.twrite("mst") if hl == 0 else []),
                            inc="V")
                        mg.append(mv)
                    for qt in range(4):
                        q0 = 512 * qt
                        exp_vals = {}
                        pend = []

                        def pslot(kc):
                            k3 = kc % 3
                            if k3 < 2:
                                return PTL[:, 1024 * k3: 1024 * k3 + 1024]
                            return PTB[:, 0:1024]

                        def emit_av(kcp, i=i):
                            for hlp in range(2):
                                psl = pslot(kcp)[:, 512*hlp: 512*hlp + 512]
                                wy = pl.twrite("b4") + pl.twrite("b5") \
                                    if (kcp, hlp) == (0, 0) else []
                                vcol = 8192 + 384*kcp + 192*i + 64*hlp
                                pl.t(lambda e, psl=psl, kcp=kcp, hlp=hlp,
                                            vcol=vcol:
                                     e.matmul(PS[4][:, :],
                                              W48[:, vcol: vcol + 128],
                                              psl,
                                              start=(kcp == 0 and hlp == 0),
                                              stop=(kcp == 15)),
                                     wait=[("S", exp_vals[kcp])] + wy)
                                ons = ONB0 if hlp == 0 else ONB32
                                av2 = pl.emit(
                                    "t", lambda e, psl=psl, kcp=kcp, hlp=hlp,
                                                ons=ons:
                                    e.matmul(PS[5][:, :],
                                             ons[:, :], psl,
                                             start=(kcp == 0 and hlp == 0),
                                             stop=(kcp == 15)),
                                    (), "T")
                            return av2

                        for kc in range(16):
                            par = kc % 2
                            for hl in range(2):
                                bank = PS[2 * hl + par]
                                rows = slice(64 * hl, 64 * hl + 64)
                                a = max(0, 128 * kc - q0)
                                has_ident = a < 512
                                wg = pl.twrite(f"b{2*hl+par}")
                                pl.t(lambda e, bank=bank, rows=rows, hl=hl, i=i,
                                            kc=kc, q0=q0, hi=has_ident:
                                     e.matmul(bank[:, :],
                                              W48[rows, 4096 + 2048*i + 128*kc:
                                                        4096 + 2048*i + 128*kc + 128],
                                              W48[rows, 2048*i + q0:
                                                        2048*i + q0 + 512],
                                              start=True, stop=not hi,
                                              tile_position=(64 * hl, 0)),
                                     wait=wg + [("V", mg[hl])])
                                if has_ident:
                                    u0 = q0 + a - 128 * kc
                                    w = 512 - a
                                    sval = pl.t(
                                        lambda e, bank=bank, a=a, u0=u0, w=w, hl=hl:
                                        e.matmul(bank[:, a:512], IDF[:, :],
                                                 MST[:, 2048*hl + u0:
                                                        2048*hl + u0 + w],
                                                 start=False, stop=True),
                                        inc="T")
                                else:
                                    sval = pl.emit("t", None, (), "T")
                            ev = pl.s(lambda e, kc=kc, par=par, q=q:
                                      e.activation(pslot(kc), SPP[par].ap()[:, :],
                                                   AF.Exp,
                                                   bias=VEC[:, 58 + q % 2:
                                                               59 + q % 2]),
                                      wait=[("T", sval)], inc="S")
                            exp_vals[kc] = ev
                            pl.consume(f"b{par}", "S", ev)
                            pl.consume(f"b{2+par}", "S", ev)
                            pend.append(kc)
                            while pend and pend[0] < kc - 1:
                                emit_av(pend.pop(0))
                        last_av = None
                        for kcp in pend:
                            last_av = emit_av(kcp)
                        sr1 = pl.s(lambda e: e.activation(RWB[0:1, 16:528],
                                                          PS[5][0:1, :], AF.Ln),
                                   wait=[("T", last_av)], inc="S")
                        sr2 = pl.s(lambda e: e.activation(RWB[0:1, 16:528],
                                                          RWB[0:1, 16:528],
                                                          AF.Exp, scale=-1.0),
                                   inc="S")
                        sr3 = pl.s(lambda e: e.activation(RWB[32:33, 16:528],
                                                          PS[5][32:33, :], AF.Ln),
                                   inc="S")
                        sr4 = pl.s(lambda e: e.activation(RWB[32:33, 16:528],
                                                          RWB[32:33, 16:528],
                                                          AF.Exp, scale=-1.0),
                                   inc="S")
                        pl.consume("b5", "S", sr3)
                        tb3 = bcast(PS[6][:, :], "b6", RWB[0:33, 16:528],
                                    EEB[:, :], 33, 512, wait=[("S", sr4)])
                        sc1 = pl.s(lambda e: e.copy(BCS[:, 0:512], PS[6][:, :]),
                                   wait=[("T", tb3)], inc="S")
                        pl.consume("b6", "S", sc1)
                        ach = 2 * q + i
                        vy = pl.v(lambda e, ach=ach, q0=q0: e.tensor_tensor(
                            ARN[:, 2048*ach + q0: 2048*ach + q0 + 512],
                            PS[4][:, :], BCS[:, 0:512], ALU.mult),
                            wait=[("S", sc1)], inc="V")
                        pl.consume("b4", "V", vy)
                    pl.consume("mst", "T", pl.cnt["T"])
                    if q == 0 and i == 0:
                        dump_mst("mst0")
                pl.barrier()

            dump_arn("yn")
            if ACUT < 4:
                return
            wp0 = pl.cnt["DG"]
            wp1 = load_w(W48[:, 0:4096], d_wproj[1])
            for hh in range(2):
                wready = wp0 if hh == 0 else wp1
                wtens = WQS if hh == 0 else W48
                wbase = 0
                for mm in range(4):
                    m = 4 * hh + mm
                    for qt in range(4):
                        bank = PS[(4 * mm + qt) % 2]
                        bname = f"b{(4 * mm + qt) % 2}"
                        tvv = None
                        for c in range(8):
                            wg = pl.twrite(bname) if c == 0 else []
                            tvv = pl.t(lambda e, bank=bank, c=c, mm=mm, qt=qt,
                                              wtens=wtens:
                                       e.matmul(bank[:, :],
                                                wtens[:, 512*c + 128*mm:
                                                         512*c + 128*mm + 128],
                                                ARN[:, 2048*c + 512*qt:
                                                       2048*c + 512*qt + 512],
                                                start=(c == 0), stop=(c == 7)),
                                       wait=[("DG", wready)] + wg,
                                       inc="T" if c == 7 else None)
                        vv = pl.v(lambda e, bank=bank, m=m, qt=qt:
                                  e.scalar_tensor_tensor(
                                      xc(m, 512*qt, 512), bank[:, :],
                                      VEC[:, 24+m:25+m], xc(m, 512*qt, 512),
                                      ALU.add, ALU.add),
                                  wait=[("T", tvv)], inc="V")
                        pl.consume(bname, "V", vv)
            pl.barrier()

        def ph_ffn():
            dr = load_w(W48[:, 0:8192], d_wr)
            dk = load_w(W48[:, 8192:16384], d_wk2)
            dv = load_w(W48[:, 16384:24576], d_wv2)
            for qt in range(4):
                q0 = 512 * qt
                xrv = {}
                for c in range(8):
                    dd = BCS[:, 512 * (c % 2): 512 * (c % 2) + 512]
                    xr = ARN[:, 512 * c: 512 * c + 512]
                    xk = ARN[:, 4096 + 512 * c: 4096 + 512 * c + 512]
                    if qt == 0:
                        pl.v(lambda e, dd=dd, c=c: e.tensor_tensor(
                            dd[:, 1:512], lnc(c, 1, 511), lnc(c, 0, 511),
                            ALU.subtract), inc="V")
                        pl.v(lambda e, xr=xr, dd=dd, c=c: e.scalar_tensor_tensor(
                            xr[:, 1:512], dd[:, 1:512], VEC[:, 32+c:33+c],
                            lnc(c, 0, 511), ALU.mult, ALU.add), inc="V")
                        pl.v(lambda e, xr=xr, c=c: e.tensor_scalar(
                            xr[:, 0:1], lnc(c, 0, 1), VEC[:, 32+c:33+c], None,
                            ALU.mult), inc="V")
                        pl.v(lambda e, xk=xk, dd=dd, c=c: e.scalar_tensor_tensor(
                            xk[:, 1:512], dd[:, 1:512], VEC[:, 40+c:41+c],
                            lnc(c, 0, 511), ALU.mult, ALU.add), inc="V")
                        va = pl.v(lambda e, xk=xk, c=c: e.tensor_scalar(
                            xk[:, 0:1], lnc(c, 0, 1), VEC[:, 40+c:41+c], None,
                            ALU.mult), inc="V")
                    else:
                        pl.v(lambda e, dd=dd, c=c, q0=q0: e.tensor_tensor(
                            dd, lnc(c, q0, 512), lnc(c, q0-1, 512),
                            ALU.subtract), inc="V")
                        pl.v(lambda e, xr=xr, dd=dd, c=c, q0=q0:
                             e.scalar_tensor_tensor(
                                 xr, dd, VEC[:, 32+c:33+c], lnc(c, q0-1, 512),
                                 ALU.mult, ALU.add), inc="V")
                        va = pl.v(lambda e, xk=xk, dd=dd, c=c, q0=q0:
                                  e.scalar_tensor_tensor(
                                      xk, dd, VEC[:, 40+c:41+c], lnc(c, q0-1, 512),
                                      ALU.mult, ALU.add), inc="V")
                    xrv[c] = va
                for m in range(8):
                    bank = PS[m % 2]
                    bname = f"b{m % 2}"
                    tvv = None
                    for c in range(8):
                        wg = pl.twrite(bname) if c == 0 else []
                        tvv = pl.t(lambda e, bank=bank, c=c, m=m:
                                   e.matmul(bank[:, :],
                                            W48[:, 1024*c + 128*m:
                                                   1024*c + 128*m + 128],
                                            ARN[:, 512*c: 512*c + 512],
                                            start=(c == 0), stop=(c == 7)),
                                   wait=[("DG", dr), ("V", xrv[c])] + wg,
                                   inc="T" if c == 7 else None)
                    sv = pl.s(lambda e, bank=bank, m=m: e.activation(
                        ARN[:, 8192 + 512*m: 8192 + 512*m + 512], bank[:, :],
                        AF.Sigmoid), wait=[("T", tvv)], inc="S")
                    pl.consume(bname, "S", sv)
                kv = {}
                for m in range(8):
                    bank = PS[2 + m % 2]
                    bname = f"b{2 + m % 2}"
                    tvv = None
                    for c in range(8):
                        wg = pl.twrite(bname) if c == 0 else []
                        tvv = pl.t(lambda e, bank=bank, c=c, m=m:
                                   e.matmul(bank[:, :],
                                            W48[:, 8192 + 1024*c + 128*m:
                                                   8192 + 1024*c + 128*m + 128],
                                            ARN[:, 4096 + 512*c: 4096 + 512*c + 512],
                                            start=(c == 0), stop=(c == 7)),
                                   wait=[("DG", dk), ("V", xrv[c])] + wg,
                                   inc="T" if c == 7 else None)
                    tmp = BCS[:, 0:512]
                    sv = pl.s(lambda e, bank=bank, tmp=tmp: e.activation(
                        tmp, bank[:, :], AF.Relu),
                        wait=[("T", tvv), ("V", kv.get(m - 1, 0))], inc="S")
                    pl.consume(bname, "S", sv)
                    vv = pl.v(lambda e, tmp=tmp, m=m: e.tensor_tensor(
                        ARN[:, 12288 + 512*m: 12288 + 512*m + 512], tmp, tmp,
                        ALU.mult), wait=[("S", sv)], inc="V")
                    kv[m] = vv
                for m in range(8):
                    bank = PS[4 + m % 2]
                    bname = f"b{4 + m % 2}"
                    tvv = None
                    for c in range(8):
                        wg = pl.twrite(bname) if c == 0 else []
                        tvv = pl.t(lambda e, bank=bank, c=c, m=m:
                                   e.matmul(bank[:, :],
                                            W48[:, 16384 + 1024*c + 128*m:
                                                   16384 + 1024*c + 128*m + 128],
                                            ARN[:, 12288 + 512*c:
                                                   12288 + 512*c + 512],
                                            start=(c == 0), stop=(c == 7)),
                                   wait=[("DG", dv), ("V", kv[c])] + wg,
                                   inc="T" if c == 7 else None)
                    tmp3 = BCS[:, 512:1024]
                    vv = pl.v(lambda e, bank=bank, m=m, tmp3=tmp3: e.tensor_tensor(
                        tmp3, bank[:, :],
                        ARN[:, 8192 + 512*m: 8192 + 512*m + 512], ALU.mult),
                        wait=[("T", tvv)], inc="V")
                    pl.consume(bname, "V", vv)
                    pl.v(lambda e, m=m, q0=q0, tmp3=tmp3: e.tensor_tensor(
                        xc(m, q0, 512), xc(m, q0, 512), tmp3, ALU.add), inc="V")
            pl.barrier()

        def ph_attn_b():
            dwc = load_w(W48[:, 0:8192], d_wc)
            dlt = load_w(W48[0:64, 12288:13312], d_ltriT)
            dq0 = load_w(WQS[:, :], d_wqkvb[0])
            pl.g(lambda e: e.memset(W48[64:128, 12288:13312], 0.0))
            pl.g(lambda e: e.memset(W48[:, 13312:14336], 0.0), inc="G")
            gz = pl.cnt["G"]
            LTT = W48.ap()[:, 12288:13312]
            BMS = W48.ap()[:, 13312:14336]
            mv = None
            for h in range(16):
                mv = pl.v(lambda e, h=h: e.tensor_tensor(
                    LTT[0:64, 64*h:64*h+64], LTT[0:64, 64*h:64*h+64],
                    TRIU[0:64, 0:64], ALU.mult),
                    wait=[("DG", dlt), ("G", gz)], inc="V")
            svb = None
            for h in range(16):
                tp = (0, 64 * (h % 2))
                rows = slice(64 * (h % 2), 64 * (h % 2) + 64)
                wg = pl.twrite(f"b{h % 2}")
                tvv = pl.t(lambda e, h=h, tp=tp, rows=rows: e.matmul(
                    PS[h % 2][rows, 0:64],
                    LTT[:, 64*h:64*h+64], LTT[:, 64*h:64*h+64],
                    start=True, stop=True, tile_position=tp),
                    wait=[("V", mv)] + wg, inc="T")
                svb = pl.s(lambda e, h=h, rows=rows: e.copy(
                    BMS[rows, 64*h:64*h+64], PS[h % 2][rows, 0:64]),
                    wait=[("T", tvv)], inc="S")
                pl.consume(f"b{h % 2}", "S", svb)
            for h in range(16):
                for mh in range(2):
                    tp = (0, 64 * (h % 2))
                    rows = slice(64 * (h % 2), 64 * (h % 2) + 64)
                    bidx = 2 + (2 * h + mh) % 2
                    bname = f"b{bidx}"
                    wg = pl.twrite(bname)
                    tvv = pl.t(lambda e, h=h, mh=mh, tp=tp, bidx=bidx, rows=rows:
                               e.matmul(
                        PS[bidx][rows, :], BMS[:, 64*h:64*h+64],
                        W48[:, 1024*(h//2) + 512*mh: 1024*(h//2) + 512*mh + 512],
                        start=True, stop=True, tile_position=tp),
                        wait=[("S", svb), ("DG", dwc)] + wg, inc="T")
                    sv = pl.s(lambda e, h=h, mh=mh, rows=rows, bidx=bidx: e.copy(
                        W48[rows, 16384 + 1024*(h//2) + 512*mh:
                                  16384 + 1024*(h//2) + 512*mh + 512],
                        PS[bidx][rows, :]),
                        wait=[("T", tvv)], inc="S")
                    pl.consume(bname, "S", sv)
            pl.barrier()

            wq_dma = dq0
            for q in range(4):
                wq_ready = wq_dma
                last_qkv_t = None
                for i in range(2):
                    for sec, base in ((0, 0), (1, 4096)):
                        for qt in range(4):
                            bank = PS[2 * sec + qt % 2]
                            bname = f"b{2 * sec + qt % 2}"
                            tvv = None
                            for c in range(8):
                                wg = pl.twrite(bname) if c == 0 else []
                                col = (c * 3 + sec) * 256 + 128 * i
                                tvv = pl.t(
                                    lambda e, bank=bank, c=c, col=col, qt=qt:
                                    e.matmul(bank[:, :], WQS[:, col:col+128],
                                             lnc(c, 512*qt, 512),
                                             start=(c == 0), stop=(c == 7)),
                                    wait=[("DG", wq_ready)] + wg,
                                    inc="T" if c == 7 else None)
                            sv = pl.s(
                                lambda e, bank=bank, base=base, i=i, qt=qt:
                                e.activation(
                                    W48[:, base + 2048*i + 512*qt:
                                           base + 2048*i + 512*qt + 512],
                                    bank[:, :], AF.Copy),
                                wait=[("T", tvv)], inc="S")
                            pl.consume(bname, "S", sv)
                            last_qkv_t = tvv
                for tt in range(16):
                    bank = PS[4 + tt % 2]
                    bname = f"b{4 + tt % 2}"
                    tvv = None
                    for c in range(8):
                        wg = pl.twrite(bname) if c == 0 else []
                        col = (c * 3 + 2) * 256
                        tvv = pl.t(
                            lambda e, bank=bank, c=c, col=col, tt=tt:
                            e.matmul(bank[:, 0:256], lnc(c, 128*tt, 128),
                                     WQS[:, col:col+256],
                                     start=(c == 0), stop=(c == 7)),
                            wait=[("DG", wq_ready)] + wg,
                            inc="T" if c == 7 else None)
                    vout = W48.ap()[:, 8192 + 384*tt: 8192 + 384*tt + 384] \
                        .rearrange("p (ip t3 r) -> p ip t3 r", ip=2, t3=3)[:, :, 0:3:2, :]
                    vv = pl.v(lambda e, bank=bank, vout=vout: e.tensor_copy(
                        vout,
                        bank[:, 0:256].rearrange("p (ip hl r) -> p ip hl r",
                                                 ip=2, hl=2)),
                        wait=[("T", tvv)], inc="V")
                    pl.consume(bname, "V", vv)
                    last_qkv_t = tvv
                if q < 3:
                    wq_dma = load_w(WQS[:, :], d_wqkvb[q + 1],
                                    wait=[("T", last_qkv_t)])
                pl.barrier()

                for i in range(2):
                    ach = 2 * q + i
                    kvevac = {}
                    for cc in range(16):
                        par = cc % 2
                        b0 = 2048 * i + 128 * cc
                        wg = pl.twrite("psb")
                        ttr = pl.t(lambda e, b0=b0, par=par: e.transpose(
                            PSB[:, 128*par: 128*par + 128],
                            W48[:, 4096 + b0: 4096 + b0 + 128], IDB[:, :]),
                            wait=wg, inc="T")
                        stk = pl.s(lambda e, par=par: e.copy(
                            KTOK[:, 128*par: 128*par + 128],
                            PSB[:, 128*par: 128*par + 128]),
                            wait=[("T", ttr)], inc="S")
                        pl.consume("psb", "S", stk)
                        smm = {}
                        for hl in range(2):
                            bank = PS[2 * hl + par]
                            rows = slice(64 * hl, 64 * hl + 64)
                            wg = pl.twrite(f"b{2*hl+par}")
                            smm[hl] = pl.t(lambda e, bank=bank, rows=rows, b0=b0,
                                                  hl=hl:
                                           e.matmul(bank[:, 0:128],
                                                    W48[rows, 4096 + b0:
                                                              4096 + b0 + 128],
                                                    W48[rows, b0: b0 + 128],
                                                    start=True, stop=True,
                                                    tile_position=(64 * hl, 0)),
                                           wait=wg, inc="T")
                        pmv = {}
                        for hl in range(2):
                            bank = PS[2 * hl + par]
                            ptb = PTB[:, 256*par + 128*hl: 256*par + 128*hl + 128]
                            pmv[hl] = pl.v(lambda e, bank=bank, ptb=ptb:
                                           e.tensor_tensor(ptb, bank[:, 0:128],
                                                           TRIU[:, :], ALU.mult),
                                           wait=[("T", smm[hl])], inc="V")
                            pl.consume(f"b{2*hl+par}", "V", pmv[hl])
                        ybank = PS[4 + par]
                        wg = pl.twrite(f"b{4 + par}")
                        for hl in range(2):
                            ptb = PTB[:, 256*par + 128*hl: 256*par + 128*hl + 128]
                            vcol = 8192 + 384*cc + 192*i + 64*hl
                            pl.t(lambda e, ybank=ybank, ptb=ptb, vcol=vcol, hl=hl:
                                 e.matmul(ybank[:, 0:128],
                                          W48[:, vcol: vcol + 128],
                                          ptb, start=(hl == 0),
                                          stop=(cc == 0 and hl == 1)),
                                 wait=[("V", pmv[hl])] + (wg if hl == 0 else []))
                            if cc > 0:
                                pl.t(lambda e, ybank=ybank, par=par, hl=hl, b0=b0:
                                     e.matmul(ybank[:, 0:128],
                                              KVSB[:, 192*par + 64*hl:
                                                      192*par + 64*hl + 128],
                                              W48[:, b0: b0 + 128],
                                              start=False, stop=(hl == 1)),
                                     wait=[("S", kvevac.get(cc - 1, 0))])
                        tlast = pl.emit("t", None, (), "T")
                        wg = pl.twrite("b6")
                        tku = None
                        for hl in range(2):
                            vcol = 8192 + 384*cc + 192*i + 128*hl
                            tku = pl.t(lambda e, par=par, hl=hl, cc=cc, vcol=vcol:
                                       e.matmul(PS[6][64*hl:64*hl+64, 0:64],
                                                KTOK[:, 128*par + 64*hl:
                                                        128*par + 64*hl + 64],
                                                W48[:, vcol: vcol + 64],
                                                start=(cc == 0),
                                                stop=True,
                                                tile_position=(0, 64 * hl)),
                                       wait=[("S", stk)] + (wg if hl == 0 else []),
                                       inc="T")
                        if cc < 15:
                            par2 = (cc + 1) % 2
                            pl.s(lambda e, par2=par2: e.copy(
                                KVSB[0:64, 192*par2: 192*par2 + 64],
                                PS[6][0:64, 0:64]), wait=[("T", tku)], inc="S")
                            sv2 = pl.s(lambda e, par2=par2: e.copy(
                                KVSB[64:128, 192*par2 + 128: 192*par2 + 192],
                                PS[6][64:128, 0:64]), inc="S")
                            kvevac[cc] = sv2
                            pl.consume("b6", "S", sv2)
                        else:
                            pl.consume("b6", "T", tku)
                        syv = pl.s(lambda e, ybank=ybank, ach=ach, cc=cc: e.copy(
                            ARN[:, 2048*ach + 128*cc: 2048*ach + 128*cc + 128],
                            ybank[:, 0:128]), wait=[("T", tlast)], inc="S")
                        pl.consume(f"b{4 + par}", "S", syv)
                pl.barrier()

            dump_arn("ynb")
            dump_w48b("wcp")
            for m in range(8):
                for qt in range(4):
                    bank = PS[(4 * m + qt) % 2]
                    bname = f"b{(4 * m + qt) % 2}"
                    tvv = None
                    for c in range(8):
                        wg = pl.twrite(bname) if c == 0 else []
                        tvv = pl.t(lambda e, bank=bank, c=c, m=m, qt=qt:
                                   e.matmul(bank[:, :],
                                            W48[:, 16384 + 1024*c + 128*m:
                                                   16384 + 1024*c + 128*m + 128],
                                            ARN[:, 2048*c + 512*qt:
                                                   2048*c + 512*qt + 512],
                                            start=(c == 0), stop=(c == 7)),
                                   wait=wg, inc="T" if c == 7 else None)
                    vv = pl.v(lambda e, bank=bank, m=m, qt=qt: e.tensor_tensor(
                        xc(m, 512*qt, 512), xc(m, 512*qt, 512), bank[:, :],
                        ALU.add), wait=[("T", tvv)], inc="V")
                    pl.consume(bname, "V", vv)
            pl.barrier()

        def dump_x(nm):
            if nm not in d_dump:
                return
            pl.barrier()
            for c in range(8):
                pl.y(lambda e, nm=nm, c=c: e.dma_start(
                    out=d_dump[nm][:, C2*c:C2*(c+1)], in_=X[:, C2*c:C2*(c+1)]),
                    inc="DO", inc_n=16, dma=True)
            pl.barrier()

        def dump_ln(nm):
            if nm not in d_dump:
                return
            pl.barrier()
            pl.g(lambda e, nm=nm: e.dma_start(out=d_dump[nm], in_=LNB[:, :]),
                 inc="DO", inc_n=16, dma=True)
            pl.barrier()

        def dump_arn(nm):
            if nm not in d_dump:
                return
            pl.barrier()
            pl.g(lambda e, nm=nm: e.dma_start(out=d_dump[nm], in_=ARN[:, :]),
                 inc="DO", inc_n=16, dma=True)
            pl.barrier()

        def dump_mst(nm):
            if nm not in d_dump:
                return
            pl.barrier()
            pl.g(lambda e, nm=nm: e.dma_start(out=d_dump[nm][:, 0:4096],
                                              in_=MST[:, :]),
                 inc="DO", inc_n=16, dma=True)
            pl.g(lambda e, nm=nm: e.dma_start(out=d_dump[nm][:, 4096:6144],
                                              in_=PTL[:, :]),
                 inc="DO", inc_n=16, dma=True)
            pl.barrier()

        def dump_w48b(nm):
            if nm not in d_dump:
                return
            pl.barrier()
            pl.g(lambda e, nm=nm: e.dma_start(out=d_dump[nm][:, 0:12288],
                                              in_=W48[:, 12288:24576]),
                 inc="DO", inc_n=16, dma=True)
            pl.barrier()

        def dump_w48(nm):
            if nm not in d_dump:
                return
            pl.barrier()
            pl.g(lambda e, nm=nm: e.dma_start(out=d_dump[nm][:, 0:12288],
                                              in_=W48[:, 0:12288]),
                 inc="DO", inc_n=16, dma=True)
            pl.barrier()

        if "ln1" in stages and "attn_a" not in stages:
            ph_layernorm()
            dump_ln("ln1")
        if "attn_a" in stages:
            ph_layernorm()
            dump_ln("ln1")
            pl.barrier()
            ph_attn_a()
            dump_x("x1")
        if "ffn1" in stages:
            ph_layernorm()
            dump_ln("ln2")
            pl.barrier()
            ph_ffn()
            dump_x("x2")
        if "attn_b" in stages:
            ph_layernorm()
            dump_ln("ln3")
            pl.barrier()
            ph_attn_b()
            dump_x("x3")
        if "ffn2" in stages:
            ph_layernorm()
            dump_ln("ln4")
            pl.barrier()
            ph_ffn()

        pl.barrier()
        for c in range(8):
            pl.y(lambda e, c=c: e.dma_start(out=d_out[128*c:128*(c+1), :],
                                            in_=X[:, C2*c:C2*(c+1)]),
                 inc="DO", inc_n=16, dma=True)
        do_final = pl.cnt["DO"]
        pl.emit("y", None, [("DO", do_final)])

        pl.finalize()
        with nc.Block() as block:
            def run_queue(key):
                def body(e):
                    for ent in pl.q[key]:
                        for k, val in ent.waits:
                            e.wait_ge(sem[k], val)
                        if ent.thunk is not None:
                            ins = ent.thunk(e)
                            if ent.inc is not None:
                                ins.then_inc(sem[ent.inc], ent.inc_n)
                        elif ent.mode == "seminc":
                            if ent.prewait is not None:
                                e.wait_ge(sem[ent.prewait[0]], ent.prewait[1])
                            e.sem_inc(sem[ent.inc], ent.inc_n)
                return body
            block.tensor(run_queue("t"))
            block.vector(run_queue("v"))
            block.scalar(run_queue("s"))
            block.gpsimd(run_queue("g"))
            block.sync(run_queue("y"))

    return nc


def _pack_inputs(L0_attn_a_x, L0_attn_a_W_qkv, L0_attn_a_b_qkv, L0_attn_a_W_proj,
                 L0_attn_a_b_proj, L0_attn_b_Wq, L0_attn_b_Wk, L0_attn_b_Wv,
                 L0_attn_b_Wc, L0_attn_b_Ltri, L0_ffn_W_r, L0_ffn_W_k, L0_ffn_W_v,
                 L0_ffn_mu_r, L0_ffn_mu_k):
    asf = lambda a: np.ascontiguousarray(np.asarray(a, dtype=np.float32))
    x = asf(L0_attn_a_x)
    W_qkv = asf(L0_attn_a_W_qkv)
    b_qkv = asf(L0_attn_a_b_qkv)
    W_proj = asf(L0_attn_a_W_proj)
    b_proj = asf(L0_attn_a_b_proj)

    def pack_qkv(w):
        a = w.reshape(8, 128, 3, 4, 256)
        return np.ascontiguousarray(a.transpose(3, 1, 0, 2, 4).reshape(4, 128, 6144))

    def pack_cm(wT):
        return np.ascontiguousarray(
            wT.reshape(8, 128, 1024).transpose(1, 0, 2).reshape(128, 8192))

    wqkv = pack_qkv(W_qkv)
    cat_b = np.ascontiguousarray(np.concatenate(
        [asf(L0_attn_b_Wq).T, asf(L0_attn_b_Wk).T, asf(L0_attn_b_Wv).T], axis=1))
    wqkvb = pack_qkv(cat_b)
    wproj = np.ascontiguousarray(
        W_proj.reshape(8, 128, 2, 512).transpose(2, 1, 0, 3).reshape(2, 128, 4096))
    wc = pack_cm(np.ascontiguousarray(asf(L0_attn_b_Wc).T))
    wr = pack_cm(np.ascontiguousarray(asf(L0_ffn_W_r).T))
    wk2 = pack_cm(np.ascontiguousarray(asf(L0_ffn_W_k).T))
    wv2 = pack_cm(np.ascontiguousarray(asf(L0_ffn_W_v).T))
    ltriT = np.ascontiguousarray(
        np.transpose(asf(L0_attn_b_Ltri), (2, 0, 1)).reshape(64, 1024))
    vecs = np.zeros((128, 48), np.float32)
    vecs[:, 0:24] = b_qkv.reshape(24, 128).T
    vecs[:, 24:32] = b_proj.reshape(8, 128).T
    vecs[:, 32:40] = asf(L0_ffn_mu_r).reshape(8, 128).T
    vecs[:, 40:48] = asf(L0_ffn_mu_k).reshape(8, 128).T
    bvrow = np.ascontiguousarray(b_qkv[2048:3072].reshape(1, 1024))
    ident = np.eye(128, dtype=np.float32)
    triu = (np.arange(128)[:, None] <= np.arange(128)[None, :]).astype(np.float32)
    iotaf = (np.arange(128)[:, None] - np.arange(2048)[None, :]).astype(np.float16)

    shared = dict(wqkv=wqkv, wqkvb=wqkvb, wproj=wproj, wc=wc, wr=wr,
                  wk2=wk2, wv2=wv2, ltriT=ltriT, vecs=vecs, bvrow=bvrow,
                  ident=ident, triu=triu, iotaf=iotaf)
    in_maps = []
    for b in range(B):
        m = dict(shared)
        m["xT"] = np.ascontiguousarray(x[b].T)
        in_maps.append(m)
    return in_maps


def kernel(**inputs):
    in_maps = _pack_inputs(**inputs)
    nc = build_program()
    res = run_bass_kernel_spmd(nc, in_maps, core_ids=list(range(B)))
    out = np.stack([np.asarray(r["out"]).T for r in res.results], axis=0)
    return np.ascontiguousarray(out.astype(np.float32))


# version 05 (final)
# speedup vs baseline: 1.0119x; 1.0119x over previous
import math
import numpy as np
import ml_dtypes
from contextlib import ExitStack

import concourse.bass as bass
import concourse.mybir as mybir
from concourse.bass_utils import run_bass_kernel_spmd

F32 = mybir.dt.float32
BF16 = mybir.dt.bfloat16
F16 = mybir.dt.float16
I16 = mybir.dt.int16
AF = mybir.ActivationFunctionType
ALU = mybir.AluOpType
AXX = mybir.AxisListType.X

B, T, C, H, Dh = 8, 2048, 1024, 16, 64
C2 = 2048


def get_slopes(n):
    def pow2(n):
        start = 2 ** (-2 ** (-(math.log2(n) - 3)))
        return [start * start ** i for i in range(n)]
    if math.log2(n).is_integer():
        return pow2(n)
    c = 2 ** math.floor(math.log2(n))
    return pow2(c) + get_slopes(2 * c)[0::2][:n - c]

SLOPES = get_slopes(H)

ALL_STAGES = ("attn_a", "ffn1", "attn_b", "ffn2")


class Entry:
    __slots__ = ("thunk", "waits", "inc", "inc_n", "dma", "mode", "prewait",
                 "planned_val")

    def __init__(self, thunk, waits, inc, inc_n, dma):
        self.thunk, self.waits = thunk, waits
        self.inc, self.inc_n, self.dma = inc, inc_n, dma
        self.mode = "op"
        self.prewait = None


class Plan:
    ENGS = ("t", "v", "s", "g", "y")
    ESEM = {"t": "T", "v": "V", "s": "S", "g": "G", "y": "Y"}

    def __init__(self):
        self.q = {e: [] for e in self.ENGS}
        self.cnt = {k: 0 for k in ("T", "V", "S", "G", "Y", "DG", "DS", "DO")}
        self.waited = {e: {} for e in self.ENGS}
        self.guards = {}

    def emit(self, eng, thunk, wait=(), inc=None, inc_n=1, dma=False):
        waits = []
        for k, val in wait:
            if val <= 0:
                continue
            if val > self.waited[eng].get(k, -1):
                waits.append((k, val))
                self.waited[eng][k] = val
        val_out = None
        if inc is not None:
            self.cnt[inc] += inc_n
            val_out = self.cnt[inc]
        ent = Entry(thunk, waits, inc, inc_n, dma)
        ent.planned_val = val_out
        self.q[eng].append(ent)
        return val_out

    def t(self, thunk, wait=(), inc=None):
        return self.emit("t", thunk, wait, inc)

    def v(self, thunk, wait=(), inc=None):
        return self.emit("v", thunk, wait, inc)

    def s(self, thunk, wait=(), inc=None):
        return self.emit("s", thunk, wait, inc)

    def g(self, thunk, wait=(), inc=None, inc_n=1, dma=False):
        return self.emit("g", thunk, wait, inc, inc_n, dma)

    def y(self, thunk, wait=(), inc=None, inc_n=1, dma=False):
        return self.emit("y", thunk, wait, inc, inc_n, dma)

    def consume(self, bank, semkey, val):
        self.guards.setdefault(bank, {})
        self.guards[bank][semkey] = max(self.guards[bank].get(semkey, 0), val)

    def twrite(self, bank):
        g = self.guards.pop(bank, {})
        return list(g.items())

    def barrier(self):
        vals = {}
        for eng in self.ENGS:
            k = self.ESEM[eng]
            if eng == "y":
                self.emit(eng, None,
                          [("DS", self.cnt["DS"]), ("DO", self.cnt["DO"])])
            vals[k] = self.emit(eng, None, (), k)
        for eng in self.ENGS:
            w = [(k, v) for k, v in vals.items() if k != self.ESEM[eng]]
            self.emit(eng, None, w)
        self.guards.clear()

    def finalize(self):
        for eng in self.ENGS:
            carrier = None
            for ent in self.q[eng]:
                if ent.thunk is not None:
                    if not ent.dma:
                        carrier = ent
                    continue
                if ent.inc is None:
                    ent.mode = "waits"
                    continue
                if carrier is not None and carrier.inc is None and not ent.waits:
                    carrier.inc, carrier.inc_n = ent.inc, ent.inc_n
                    carrier.planned_val = ent.planned_val
                    ent.mode = "waits"
                elif carrier is not None and carrier.inc is not None:
                    ent.mode = "seminc"
                    ent.prewait = (carrier.inc, self._planned_val(carrier))
                else:
                    ent.mode = "seminc"

    def _planned_val(self, ent):
        return ent.planned_val


def build_program(stages=ALL_STAGES, dumps=()):
    nc = bass.Bass("TRN2", target_bir_lowering=False)

    d_xT = nc.dram_tensor("xT", [C, T], F32, kind="ExternalInput").ap()
    d_wqkv = nc.dram_tensor("wqkv", [4, 128, 6144], F32, kind="ExternalInput").ap()
    d_wqkvb = nc.dram_tensor("wqkvb", [4, 128, 6144], F32, kind="ExternalInput").ap()
    d_wproj = nc.dram_tensor("wproj", [2, 128, 4096], F32, kind="ExternalInput").ap()
    d_wc = nc.dram_tensor("wc", [128, 8192], F32, kind="ExternalInput").ap()
    d_wr = nc.dram_tensor("wr", [128, 8192], F32, kind="ExternalInput").ap()
    d_wk2 = nc.dram_tensor("wk2", [128, 8192], F32, kind="ExternalInput").ap()
    d_wv2 = nc.dram_tensor("wv2", [128, 8192], F32, kind="ExternalInput").ap()
    d_ltriT = nc.dram_tensor("ltriT", [64, 1024], F32, kind="ExternalInput").ap()
    d_vecs = nc.dram_tensor("vecs", [128, 48], F32, kind="ExternalInput").ap()
    d_ident = nc.dram_tensor("ident", [128, 128], F32, kind="ExternalInput").ap()
    d_triu = nc.dram_tensor("triu", [128, 128], F32, kind="ExternalInput").ap()
    d_iotaf = nc.dram_tensor("iotaf", [128, 2048], F16, kind="ExternalInput").ap()
    d_onbs = nc.dram_tensor("onbs", [128, 256], F32, kind="ExternalInput").ap()
    d_bvrow = nc.dram_tensor("bvrow", [1, 1024], F32, kind="ExternalInput").ap()
    d_out = nc.dram_tensor("out", [C, T], F32, kind="ExternalOutput").ap()
    d_dump = {}
    for nm in dumps:
        d_dump[nm] = nc.dram_tensor("dump_" + nm, [128, 16384], F32,
                                    kind="ExternalOutput").ap()

    pl = Plan()

    with ExitStack() as ctx:
        SBT = lambda n, sh, dt: ctx.enter_context(nc.sbuf_tensor(n, sh, dt))
        PST = lambda n, sh, dt: ctx.enter_context(nc.psum_tensor(n, sh, dt))

        X = SBT("X", [128, 16384], F32)
        LNB = SBT("LNB", [128, 16384], BF16)
        ARN = SBT("ARN", [128, 16384], BF16)
        W48 = SBT("W48", [128, 24576], BF16)
        WQS = SBT("WQS", [128, 6144], BF16)
        BCS = SBT("BCS", [128, 1024], F32)
        RWB = SBT("RWB", [128, 2048], F32)
        VEC = SBT("VEC", [128, 64], F32)
        KTOK = SBT("KTOK", [128, 256], BF16)
        KVSB = SBT("KVSB", [128, 384], BF16)
        PTB = SBT("PTB", [128, 1024], BF16)
        E0B = SBT("E0B", [128, 128], BF16)
        EEB = SBT("EEB", [128, 128], BF16)
        RB16 = SBT("RB16", [128, 1024], BF16)
        TRIU = SBT("TRIU", [128, 128], F32)
        IDF = SBT("IDF", [128, 128], F16)
        IDB = SBT("IDB", [128, 128], BF16)
        ONB = SBT("ONB", [128, 8], BF16)
        ONBS = SBT("ONBS", [128, 256], BF16)
        ONB0 = ONBS.ap()[:, 0:128]
        ONB32 = ONBS.ap()[:, 128:256]

        MST = W48.ap()[:, 14336:18432].bitcast(F16)
        IOTA = W48.ap()[:, 18432:20480].bitcast(F16)
        PTL = W48.ap()[:, 20480:24576]

        SPP = [PST("SPP0", [128, 1024], F32), PST("SPP1", [128, 1024], F32)]
        PS47 = [PST(f"PS{i}", [128, 512], F32) for i in range(3)]
        PS = [SPP[0].ap()[:, 0:512], SPP[1].ap()[:, 0:512],
              SPP[0].ap()[:, 512:1024], SPP[1].ap()[:, 512:1024],
              PS47[0].ap()[:, :], PS47[1].ap()[:, :], PS47[2].ap()[:, :]]
        PSB = PST("PSB", [128, 1024], BF16)

        sem = {k: ctx.enter_context(nc.semaphore("sem" + k))
               for k in ("T", "V", "S", "G", "Y", "DG", "DS", "DO")}

        def xc(c, q0=0, w=C2):
            return X[:, C2 * c + q0: C2 * c + q0 + w]

        def lnc(c, q0=0, w=C2):
            return LNB[:, C2 * c + q0: C2 * c + q0 + w]

        def load_w(dst_ap, src_ap, wait=()):
            return pl.g(lambda e: e.dma_start(out=dst_ap, in_=src_ap),
                        wait=wait, inc="DG", inc_n=16, dma=True)

        def bcast(bank_ap, bname, src_ap, sel, nrows, w, wait=()):
            hi = RB16[0:nrows, 0:w]
            lo = RB16[0:nrows, 512:512 + w]
            v1 = pl.v(lambda e: e.tensor_copy(hi, src_ap),
                      wait=list(wait) + pl.twrite("rb16"), inc="V")
            v2 = pl.v(lambda e: e.tensor_tensor(lo, src_ap, hi, ALU.subtract),
                      wait=[("V", v1)], inc="V")
            wg = pl.twrite(bname)
            pl.t(lambda e: e.matmul(bank_ap, sel, RB16[:, 0:w],
                                    start=True, stop=False),
                 wait=[("V", v2)] + wg)
            tb = pl.t(lambda e: e.matmul(bank_ap, sel, RB16[:, 512:512 + w],
                                         start=False, stop=True), inc="T")
            pl.consume("rb16", "T", tb)
            return tb

        for c in range(8):
            pl.y(lambda e, c=c: e.dma_start(out=X[:, C2*c:C2*(c+1)],
                                            in_=d_xT[128*c:128*(c+1), :]),
                 inc="DS", inc_n=16, dma=True)
        pl.y(lambda e: e.dma_start(out=VEC[:, 0:48], in_=d_vecs), inc="DS", inc_n=16, dma=True)
        pl.y(lambda e: e.dma_start(out=RWB[0:1, 1024:2048], in_=d_bvrow),
             inc="DS", inc_n=16, dma=True)
        dma_init = pl.cnt["DS"]

        pl.g(lambda e: e.dma_start(out=IOTA, in_=d_iotaf), inc="DG", inc_n=16,
             dma=True)
        pl.g(lambda e: e.dma_start(out=IDF[:, :], in_=d_ident), inc="DG",
             inc_n=16, dma=True)
        pl.g(lambda e: e.dma_start(out=IDB[:, :], in_=d_ident), inc="DG",
             inc_n=16, dma=True)
        pl.y(lambda e: e.dma_start(out=TRIU[:, :], in_=d_triu), inc="DS",
             inc_n=16, dma=True)
        pl.g(lambda e: e.memset(ONB[:, :], 1.0))
        pl.g(lambda e: e.dma_start(out=ONBS[:, :], in_=d_onbs), inc="DG",
             inc_n=16, dma=True)
        pl.g(lambda e: e.memset(W48[:, 8192:14336], 0.0))
        pl.g(lambda e: e.memset(E0B[:, :], 0.0))
        pl.g(lambda e: e.memset(E0B[0:1, :], 1.0))
        pl.g(lambda e: e.memset(EEB[:, :], 0.0))
        pl.g(lambda e: e.memset(EEB[0:1, 0:64], 1.0))
        pl.g(lambda e: e.memset(EEB[32:33, 64:128], 1.0))
        pl.g(lambda e: e.memset(RB16[:, :], 0.0))
        pl.g(lambda e: e.memset(RWB[:, 0:1024], 0.0))
        pl.g(lambda e: e.memset(KVSB[:, :], 0.0))
        pl.g(lambda e: e.memset(VEC[:, 57:58], 1e-5), inc="G")
        pl.s(lambda e: e.mul(VEC[:, 49:57], VEC[:, 0:8], 0.125),
             wait=[("DS", dma_init)], inc="S")
        load_w(WQS[:, :], d_wqkv[0])
        pl.barrier()

        import os as _os
        LN_CUT = int(_os.environ.get("LN_CUT", "9"))

        def ph_layernorm():
            bsum, bsq, binv, bnm = PS[0], PS[1], PS[2], PS[3]
            smm = {}
            for qt in range(4):
                q0 = 512 * qt
                tv = None
                for c in range(8):
                    gi = 8 * qt + c
                    xb = ARN[:, 8192 + 1024 * (c % 2): 8192 + 1024 * (c % 2) + 512]
                    sq = ARN[:, 12288 + 1024 * (c % 2): 12288 + 1024 * (c % 2) + 512]
                    wbuf = [("T", smm[gi - 2])] if gi - 2 in smm else []
                    vv = pl.v(lambda e, xb=xb, c=c, q0=q0:
                              e.tensor_copy(xb, xc(c, q0, 512)),
                              wait=wbuf, inc="V")
                    ss = pl.s(lambda e, sq=sq, c=c, q0=q0:
                              e.activation(sq, xc(c, q0, 512), AF.Square),
                              wait=wbuf, inc="S")
                    wg = (pl.twrite("b0") + pl.twrite("b1")) if c == 0 else []
                    pl.t(lambda e, xb=xb, c=c:
                         e.matmul(bsum[0:1, :], ONB[:, 0:1], xb,
                                  start=(c == 0), stop=(c == 7)),
                         wait=[("V", vv)] + wg)
                    tv = pl.t(lambda e, sq=sq, c=c:
                              e.matmul(bsq[0:1, :], ONB[:, 0:1], sq,
                                       start=(c == 0), stop=(c == 7)),
                              wait=[("S", ss)], inc="T")
                    smm[gi] = tv
                if LN_CUT < 2:
                    continue
                mu = RWB[0:1, 0:512]
                m2 = RWB[0:1, 512:1024]
                v1 = pl.v(lambda e: e.tensor_scalar(mu, bsum[0:1, :], 1.0 / 1024,
                                                    None, ALU.mult),
                          wait=[("T", tv)], inc="V")
                pl.consume("b0", "V", v1)
                v2 = pl.v(lambda e: e.tensor_tensor(m2, mu, mu, ALU.mult),
                          wait=[("V", v1)], inc="V")
                v3 = pl.v(lambda e: e.scalar_tensor_tensor(m2, bsq[0:1, :],
                                                           1.0 / 1024, m2,
                                                           ALU.mult, ALU.subtract),
                          wait=[("V", v2), ("T", tv)], inc="V")
                pl.consume("b1", "V", v3)
                if LN_CUT < 3:
                    continue
                s1 = pl.s(lambda e: e.activation(m2, m2, AF.Ln,
                                                 bias=VEC[0:1, 57:58]),
                          wait=[("V", v3)], inc="S")
                s1b = pl.s(lambda e: e.activation(m2, m2, AF.Exp, scale=-0.5),
                           wait=[("S", s1)], inc="S")
                v4 = s1b

                v5 = pl.v(lambda e: e.scalar_tensor_tensor(mu, mu, -1.0, m2,
                                                           ALU.mult, ALU.mult),
                          wait=[("S", v4)], inc="V")
                if LN_CUT < 4:
                    continue
                bcast(binv[:, :], "b2", RWB[0:1, 512:1024], E0B[:, :], 1, 512,
                      wait=[("S", v4)])
                tb = bcast(bnm[:, :], "b3", RWB[0:1, 0:512], E0B[:, :], 1, 512,
                           wait=[("V", v5)])
                if LN_CUT < 5:
                    continue
                vn = None
                for c in range(8):
                    tsb = BCS[:, 512 * (c % 2): 512 * (c % 2) + 512]
                    pl.v(lambda e, tsb=tsb, c=c, q0=q0:
                         e.tensor_tensor(tsb, xc(c, q0, 512), binv[:, :], ALU.mult),
                         wait=[("T", tb)], inc="V")
                    vn = pl.v(lambda e, tsb=tsb, c=c, q0=q0:
                              e.tensor_tensor(lnc(c, q0, 512), tsb, bnm[:, :],
                                              ALU.add), inc="V")
                pl.consume("b2", "V", vn)
                pl.consume("b3", "V", vn)

        ACUT = int(_os.environ.get("ATTN_CUT", "9"))

        def ph_attn_a():
            wq_dma = pl.cnt["DG"]
            for q in range(4):
                wq_ready = wq_dma
                last_qkv_t = None
                for i in range(2):
                    gqc = 2 * q + i
                    for sec, base, bias_col, scale in (
                            (0, 0, 49 + gqc, 0.125), (1, 4096, 8 + gqc, 1.0)):
                        for qt in range(4):
                            bank = PS[2 * sec + qt % 2]
                            bname = f"b{2 * sec + qt % 2}"
                            tvv = None
                            for c in range(8):
                                wg = pl.twrite(bname) if c == 0 else []
                                col = (c * 3 + sec) * 256 + 128 * i
                                tvv = pl.t(
                                    lambda e, bank=bank, c=c, col=col, q0=512*qt:
                                    e.matmul(bank[:, :], WQS[:, col:col+128],
                                             lnc(c, q0, 512),
                                             start=(c == 0), stop=(c == 7)),
                                    wait=[("DG", wq_ready)] + wg,
                                    inc="T" if c == 7 else None)
                            sv = pl.s(
                                lambda e, bank=bank, base=base, i=i, qt=qt,
                                       bc=bias_col, sc=scale:
                                e.activation(
                                    W48[:, base + 2048*i + 512*qt:
                                           base + 2048*i + 512*qt + 512],
                                    bank[:, :], AF.Identity,
                                    bias=VEC[:, bc:bc+1], scale=sc),
                                wait=[("T", tvv)], inc="S")
                            pl.consume(bname, "S", sv)
                            last_qkv_t = tvv
                tb = bcast(PS[6][:, 0:256], "b6",
                           RWB[0:1, 1024 + 256*q: 1024 + 256*q + 256],
                           E0B[:, :], 1, 256)
                sb = pl.s(lambda e: e.copy(BCS[:, 0:256], PS[6][:, 0:256]),
                          wait=[("T", tb)], inc="S")
                pl.consume("b6", "S", sb)
                for tt in range(16):
                    bank = PS[4 + tt % 2]
                    bname = f"b{4 + tt % 2}"
                    tvv = None
                    for c in range(8):
                        wg = pl.twrite(bname) if c == 0 else []
                        col = (c * 3 + 2) * 256
                        tvv = pl.t(
                            lambda e, bank=bank, c=c, col=col, tt=tt:
                            e.matmul(bank[:, 0:256], lnc(c, 128*tt, 128),
                                     WQS[:, col:col+256],
                                     start=(c == 0), stop=(c == 7)),
                            wait=[("DG", wq_ready)] + wg,
                            inc="T" if c == 7 else None)
                    vout = W48.ap()[:, 8192 + 384*tt: 8192 + 384*tt + 384] \
                        .rearrange("p (ip t3 r) -> p ip t3 r", ip=2, t3=3)[:, :, 0:3:2, :]
                    vv = pl.v(lambda e, bank=bank, vout=vout: e.tensor_tensor(
                        vout,
                        bank[:, 0:256].rearrange("p (ip hl r) -> p ip hl r", ip=2, hl=2),
                        BCS.ap()[:, 0:256].rearrange("p (ip hl r) -> p ip hl r",
                                                     ip=2, hl=2)[:, :, :, :],
                        ALU.add),
                        wait=[("T", tvv), ("S", sb)], inc="V")
                    pl.consume(bname, "V", vv)
                    last_qkv_t = tvv
                if q == 0:
                    dump_w48("w48q0")
                if q < 3:
                    wq_dma = load_w(WQS[:, :], d_wqkv[q + 1],
                                    wait=[("T", last_qkv_t)])
                else:
                    load_w(WQS[:, 0:4096], d_wproj[0],
                           wait=[("T", last_qkv_t)])
                pl.barrier()

                if ACUT < 2:
                    continue
                prev_stats_t = 0
                for mi, base in ((0, 0), (1, 4096)):
                    for i in range(2):
                        vv = pl.v(lambda e, base=base, i=i: e.tensor_tensor(
                            PTL[:, 0:2048],
                            W48[:, base + 2048*i: base + 2048*i + 2048],
                            W48[:, base + 2048*i: base + 2048*i + 2048], ALU.mult),
                            wait=[("T", prev_stats_t)], inc="V")
                        for qt in range(4):
                            wg = pl.twrite(f"b{qt}") if i == 0 else []
                            tvv = pl.t(lambda e, qt=qt, i=i: e.matmul(
                                PS[qt][0:1, :], ONB[:, 0:1],
                                PTL[:, 512*qt: 512*qt + 512],
                                start=(i == 0), stop=(i == 1)),
                                wait=[("V", vv)] + wg,
                                inc="T")
                            prev_stats_t = tvv
                            if i == 1:
                                vv2 = pl.v(lambda e, qt=qt, mi=mi: e.reduce_max(
                                    RWB[0:1, 4*mi + qt: 4*mi + qt + 1],
                                    PS[qt][0:1, :], axis=AXX),
                                    wait=[("T", tvv)], inc="V")
                                pl.consume(f"b{qt}", "V", vv2)
                v6 = pl.v(lambda e: e.reduce_max(RWB[0:1, 8:9], RWB[0:1, 0:4],
                                                 axis=AXX), inc="V")
                v7 = pl.v(lambda e: e.reduce_max(RWB[0:1, 9:10], RWB[0:1, 4:8],
                                                 axis=AXX), wait=[("V", v6)],
                          inc="V")
                v8 = pl.v(lambda e: e.tensor_tensor(RWB[0:1, 8:9], RWB[0:1, 8:9],
                                                    RWB[0:1, 9:10], ALU.mult),
                          wait=[("V", v7)], inc="V")
                s2 = pl.s(lambda e: e.activation(RWB[0:1, 8:9], RWB[0:1, 8:9],
                                                 AF.Sqrt),
                          wait=[("V", v8)], inc="S")
                v9 = pl.v(lambda e: e.tensor_scalar(RWB[0:1, 8:9], RWB[0:1, 8:9],
                                                    -1.0, None, ALU.mult),
                          wait=[("S", s2)], inc="V")
                tb2 = bcast(PS[6][:, 0:1], "b6", RWB[0:1, 8:9], E0B[:, :], 1, 1,
                            wait=[("V", v9)])
                sm = pl.s(lambda e, q=q: e.copy(VEC[:, 58 + q % 2: 59 + q % 2],
                                                PS[6][:, 0:1]),
                          wait=[("T", tb2)], inc="S")
                pl.consume("b6", "S", sm)

                if ACUT < 3:
                    continue
                for i in range(2):
                    mg = []
                    for hl in range(2):
                        slope = SLOPES[4 * q + 2 * i + hl]
                        mv = pl.v(lambda e, hl=hl, slope=slope: e.tensor_scalar(
                            MST[:, 2048*hl: 2048*hl + 2048], IOTA, 0.0, slope,
                            ALU.min, ALU.mult),
                            wait=[("S", sm), ("DG", 48)] +
                                 (pl.twrite("mst") if hl == 0 else []),
                            inc="V")
                        mg.append(mv)
                    for qt in range(4):
                        q0 = 512 * qt
                        exp_vals = {}
                        pend = []

                        def pslot(kc):
                            k3 = kc % 3
                            if k3 < 2:
                                return PTL[:, 1024 * k3: 1024 * k3 + 1024]
                            return PTB[:, 0:1024]

                        def emit_av(kcp, i=i):
                            for hlp in range(2):
                                psl = pslot(kcp)[:, 512*hlp: 512*hlp + 512]
                                wy = pl.twrite("b4") + pl.twrite("b5") \
                                    if (kcp, hlp) == (0, 0) else []
                                vcol = 8192 + 384*kcp + 192*i + 64*hlp
                                pl.t(lambda e, psl=psl, kcp=kcp, hlp=hlp,
                                            vcol=vcol:
                                     e.matmul(PS[4][:, :],
                                              W48[:, vcol: vcol + 128],
                                              psl,
                                              start=(kcp == 0 and hlp == 0),
                                              stop=(kcp == 15)),
                                     wait=[("S", exp_vals[kcp])] + wy)
                                ons = ONB0 if hlp == 0 else ONB32
                                av2 = pl.emit(
                                    "t", lambda e, psl=psl, kcp=kcp, hlp=hlp,
                                                ons=ons:
                                    e.matmul(PS[5][:, :],
                                             ons, psl,
                                             start=(kcp == 0 and hlp == 0),
                                             stop=(kcp == 15)),
                                    (), "T")
                            return av2

                        for kc in range(16):
                            par = kc % 2
                            for hl in range(2):
                                bank = PS[2 * hl + par]
                                rows = slice(64 * hl, 64 * hl + 64)
                                a = max(0, 128 * kc - q0)
                                has_ident = a < 512
                                wg = pl.twrite(f"b{2*hl+par}")
                                pl.t(lambda e, bank=bank, rows=rows, hl=hl, i=i,
                                            kc=kc, q0=q0, hi=has_ident:
                                     e.matmul(bank[:, :],
                                              W48[rows, 4096 + 2048*i + 128*kc:
                                                        4096 + 2048*i + 128*kc + 128],
                                              W48[rows, 2048*i + q0:
                                                        2048*i + q0 + 512],
                                              start=True, stop=not hi,
                                              tile_position=(64 * hl, 0)),
                                     wait=wg + [("V", mg[hl])])
                                if has_ident:
                                    u0 = q0 + a - 128 * kc
                                    w = 512 - a
                                    sval = pl.t(
                                        lambda e, bank=bank, a=a, u0=u0, w=w, hl=hl:
                                        e.matmul(bank[:, a:512], IDF[:, :],
                                                 MST[:, 2048*hl + u0:
                                                        2048*hl + u0 + w],
                                                 start=False, stop=True),
                                        inc="T")
                                else:
                                    sval = pl.emit("t", None, (), "T")
                            ev = pl.s(lambda e, kc=kc, par=par, q=q:
                                      e.activation(pslot(kc), SPP[par].ap()[:, :],
                                                   AF.Exp,
                                                   bias=VEC[:, 58 + q % 2:
                                                               59 + q % 2]),
                                      wait=[("T", sval)], inc="S")
                            exp_vals[kc] = ev
                            pl.consume(f"b{par}", "S", ev)
                            pl.consume(f"b{2+par}", "S", ev)
                            pend.append(kc)
                            while pend and pend[0] < kc - 1:
                                emit_av(pend.pop(0))
                        last_av = None
                        for kcp in pend:
                            last_av = emit_av(kcp)
                        sr1 = pl.s(lambda e: e.activation(RWB[0:1, 16:528],
                                                          PS[5][0:1, :], AF.Ln),
                                   wait=[("T", last_av)], inc="S")
                        sr2 = pl.s(lambda e: e.activation(RWB[0:1, 16:528],
                                                          RWB[0:1, 16:528],
                                                          AF.Exp, scale=-1.0),
                                   inc="S")
                        sr3 = pl.s(lambda e: e.activation(RWB[32:33, 16:528],
                                                          PS[5][32:33, :], AF.Ln),
                                   inc="S")
                        sr4 = pl.s(lambda e: e.activation(RWB[32:33, 16:528],
                                                          RWB[32:33, 16:528],
                                                          AF.Exp, scale=-1.0),
                                   inc="S")
                        pl.consume("b5", "S", sr3)
                        tb3 = bcast(PS[6][:, :], "b6", RWB[0:33, 16:528],
                                    EEB[:, :], 33, 512, wait=[("S", sr4)])
                        sc1 = pl.s(lambda e: e.copy(BCS[:, 0:512], PS[6][:, :]),
                                   wait=[("T", tb3)], inc="S")
                        pl.consume("b6", "S", sc1)
                        ach = 2 * q + i
                        vy = pl.v(lambda e, ach=ach, q0=q0: e.tensor_tensor(
                            ARN[:, 2048*ach + q0: 2048*ach + q0 + 512],
                            PS[4][:, :], BCS[:, 0:512], ALU.mult),
                            wait=[("S", sc1)], inc="V")
                        pl.consume("b4", "V", vy)
                    pl.consume("mst", "T", pl.cnt["T"])
                    if q == 0 and i == 0:
                        dump_mst("mst0")
                pl.barrier()

            dump_arn("yn")
            if ACUT < 4:
                return
            wp0 = pl.cnt["DG"]
            wp1 = load_w(W48[:, 0:4096], d_wproj[1])
            for hh in range(2):
                wready = wp0 if hh == 0 else wp1
                wtens = WQS if hh == 0 else W48
                wbase = 0
                for mm in range(4):
                    m = 4 * hh + mm
                    for qt in range(4):
                        bank = PS[(4 * mm + qt) % 2]
                        bname = f"b{(4 * mm + qt) % 2}"
                        tvv = None
                        for c in range(8):
                            wg = pl.twrite(bname) if c == 0 else []
                            tvv = pl.t(lambda e, bank=bank, c=c, mm=mm, qt=qt,
                                              wtens=wtens:
                                       e.matmul(bank[:, :],
                                                wtens[:, 512*c + 128*mm:
                                                         512*c + 128*mm + 128],
                                                ARN[:, 2048*c + 512*qt:
                                                       2048*c + 512*qt + 512],
                                                start=(c == 0), stop=(c == 7)),
                                       wait=[("DG", wready)] + wg,
                                       inc="T" if c == 7 else None)
                        vv = pl.v(lambda e, bank=bank, m=m, qt=qt:
                                  e.scalar_tensor_tensor(
                                      xc(m, 512*qt, 512), bank[:, :],
                                      VEC[:, 24+m:25+m], xc(m, 512*qt, 512),
                                      ALU.add, ALU.add),
                                  wait=[("T", tvv)], inc="V")
                        pl.consume(bname, "V", vv)
            pl.barrier()

        def ph_ffn():
            dr = load_w(W48[:, 0:8192], d_wr)
            dk = load_w(W48[:, 8192:16384], d_wk2)
            dv = load_w(W48[:, 16384:24576], d_wv2)
            for qt in range(4):
                q0 = 512 * qt
                xrv = {}
                for c in range(8):
                    dd = BCS[:, 512 * (c % 2): 512 * (c % 2) + 512]
                    xr = ARN[:, 512 * c: 512 * c + 512]
                    xk = ARN[:, 4096 + 512 * c: 4096 + 512 * c + 512]
                    if qt == 0:
                        pl.v(lambda e, dd=dd, c=c: e.tensor_tensor(
                            dd[:, 1:512], lnc(c, 1, 511), lnc(c, 0, 511),
                            ALU.subtract), inc="V")
                        pl.v(lambda e, xr=xr, dd=dd, c=c: e.scalar_tensor_tensor(
                            xr[:, 1:512], dd[:, 1:512], VEC[:, 32+c:33+c],
                            lnc(c, 0, 511), ALU.mult, ALU.add), inc="V")
                        pl.v(lambda e, xr=xr, c=c: e.tensor_scalar(
                            xr[:, 0:1], lnc(c, 0, 1), VEC[:, 32+c:33+c], None,
                            ALU.mult), inc="V")
                        pl.v(lambda e, xk=xk, dd=dd, c=c: e.scalar_tensor_tensor(
                            xk[:, 1:512], dd[:, 1:512], VEC[:, 40+c:41+c],
                            lnc(c, 0, 511), ALU.mult, ALU.add), inc="V")
                        va = pl.v(lambda e, xk=xk, c=c: e.tensor_scalar(
                            xk[:, 0:1], lnc(c, 0, 1), VEC[:, 40+c:41+c], None,
                            ALU.mult), inc="V")
                    else:
                        pl.v(lambda e, dd=dd, c=c, q0=q0: e.tensor_tensor(
                            dd, lnc(c, q0, 512), lnc(c, q0-1, 512),
                            ALU.subtract), inc="V")
                        pl.v(lambda e, xr=xr, dd=dd, c=c, q0=q0:
                             e.scalar_tensor_tensor(
                                 xr, dd, VEC[:, 32+c:33+c], lnc(c, q0-1, 512),
                                 ALU.mult, ALU.add), inc="V")
                        va = pl.v(lambda e, xk=xk, dd=dd, c=c, q0=q0:
                                  e.scalar_tensor_tensor(
                                      xk, dd, VEC[:, 40+c:41+c], lnc(c, q0-1, 512),
                                      ALU.mult, ALU.add), inc="V")
                    xrv[c] = va
                for m in range(8):
                    bank = PS[m % 2]
                    bname = f"b{m % 2}"
                    tvv = None
                    for c in range(8):
                        wg = pl.twrite(bname) if c == 0 else []
                        tvv = pl.t(lambda e, bank=bank, c=c, m=m:
                                   e.matmul(bank[:, :],
                                            W48[:, 1024*c + 128*m:
                                                   1024*c + 128*m + 128],
                                            ARN[:, 512*c: 512*c + 512],
                                            start=(c == 0), stop=(c == 7)),
                                   wait=[("DG", dr), ("V", xrv[c])] + wg,
                                   inc="T" if c == 7 else None)
                    sv = pl.s(lambda e, bank=bank, m=m: e.activation(
                        ARN[:, 8192 + 512*m: 8192 + 512*m + 512], bank[:, :],
                        AF.Sigmoid), wait=[("T", tvv)], inc="S")
                    pl.consume(bname, "S", sv)
                kv = {}
                for m in range(8):
                    bank = PS[2 + m % 2]
                    bname = f"b{2 + m % 2}"
                    tvv = None
                    for c in range(8):
                        wg = pl.twrite(bname) if c == 0 else []
                        tvv = pl.t(lambda e, bank=bank, c=c, m=m:
                                   e.matmul(bank[:, :],
                                            W48[:, 8192 + 1024*c + 128*m:
                                                   8192 + 1024*c + 128*m + 128],
                                            ARN[:, 4096 + 512*c: 4096 + 512*c + 512],
                                            start=(c == 0), stop=(c == 7)),
                                   wait=[("DG", dk), ("V", xrv[c])] + wg,
                                   inc="T" if c == 7 else None)
                    tmp = BCS[:, 0:512]
                    sv = pl.s(lambda e, bank=bank, tmp=tmp: e.activation(
                        tmp, bank[:, :], AF.Relu),
                        wait=[("T", tvv), ("V", kv.get(m - 1, 0))], inc="S")
                    pl.consume(bname, "S", sv)
                    vv = pl.v(lambda e, tmp=tmp, m=m: e.tensor_tensor(
                        ARN[:, 12288 + 512*m: 12288 + 512*m + 512], tmp, tmp,
                        ALU.mult), wait=[("S", sv)], inc="V")
                    kv[m] = vv
                for m in range(8):
                    bank = PS[4 + m % 2]
                    bname = f"b{4 + m % 2}"
                    tvv = None
                    for c in range(8):
                        wg = pl.twrite(bname) if c == 0 else []
                        tvv = pl.t(lambda e, bank=bank, c=c, m=m:
                                   e.matmul(bank[:, :],
                                            W48[:, 16384 + 1024*c + 128*m:
                                                   16384 + 1024*c + 128*m + 128],
                                            ARN[:, 12288 + 512*c:
                                                   12288 + 512*c + 512],
                                            start=(c == 0), stop=(c == 7)),
                                   wait=[("DG", dv), ("V", kv[c])] + wg,
                                   inc="T" if c == 7 else None)
                    tmp3 = BCS[:, 512:1024]
                    vv = pl.v(lambda e, bank=bank, m=m, tmp3=tmp3: e.tensor_tensor(
                        tmp3, bank[:, :],
                        ARN[:, 8192 + 512*m: 8192 + 512*m + 512], ALU.mult),
                        wait=[("T", tvv)], inc="V")
                    pl.consume(bname, "V", vv)
                    pl.v(lambda e, m=m, q0=q0, tmp3=tmp3: e.tensor_tensor(
                        xc(m, q0, 512), xc(m, q0, 512), tmp3, ALU.add), inc="V")
            pl.barrier()

        def ph_attn_b():
            dwc = load_w(W48[:, 0:8192], d_wc)
            dlt = load_w(W48[0:64, 12288:13312], d_ltriT)
            dq0 = load_w(WQS[:, :], d_wqkvb[0])
            pl.g(lambda e: e.memset(W48[64:128, 12288:13312], 0.0))
            pl.g(lambda e: e.memset(W48[:, 13312:14336], 0.0), inc="G")
            gz = pl.cnt["G"]
            LTT = W48.ap()[:, 12288:13312]
            BMS = W48.ap()[:, 13312:14336]
            mv = None
            for h in range(16):
                mv = pl.v(lambda e, h=h: e.tensor_tensor(
                    LTT[0:64, 64*h:64*h+64], LTT[0:64, 64*h:64*h+64],
                    TRIU[0:64, 0:64], ALU.mult),
                    wait=[("DG", dlt), ("G", gz)], inc="V")
            svb = None
            for h in range(16):
                tp = (0, 64 * (h % 2))
                rows = slice(64 * (h % 2), 64 * (h % 2) + 64)
                wg = pl.twrite(f"b{h % 2}")
                tvv = pl.t(lambda e, h=h, tp=tp, rows=rows: e.matmul(
                    PS[h % 2][rows, 0:64],
                    LTT[:, 64*h:64*h+64], LTT[:, 64*h:64*h+64],
                    start=True, stop=True, tile_position=tp),
                    wait=[("V", mv)] + wg, inc="T")
                svb = pl.s(lambda e, h=h, rows=rows: e.copy(
                    BMS[rows, 64*h:64*h+64], PS[h % 2][rows, 0:64]),
                    wait=[("T", tvv)], inc="S")
                pl.consume(f"b{h % 2}", "S", svb)
            for h in range(16):
                for mh in range(2):
                    tp = (0, 64 * (h % 2))
                    rows = slice(64 * (h % 2), 64 * (h % 2) + 64)
                    bidx = 2 + (2 * h + mh) % 2
                    bname = f"b{bidx}"
                    wg = pl.twrite(bname)
                    tvv = pl.t(lambda e, h=h, mh=mh, tp=tp, bidx=bidx, rows=rows:
                               e.matmul(
                        PS[bidx][rows, :], BMS[:, 64*h:64*h+64],
                        W48[:, 1024*(h//2) + 512*mh: 1024*(h//2) + 512*mh + 512],
                        start=True, stop=True, tile_position=tp),
                        wait=[("S", svb), ("DG", dwc)] + wg, inc="T")
                    sv = pl.s(lambda e, h=h, mh=mh, rows=rows, bidx=bidx: e.copy(
                        W48[rows, 16384 + 1024*(h//2) + 512*mh:
                                  16384 + 1024*(h//2) + 512*mh + 512],
                        PS[bidx][rows, :]),
                        wait=[("T", tvv)], inc="S")
                    pl.consume(bname, "S", sv)
            pl.barrier()

            wq_dma = dq0
            for q in range(4):
                wq_ready = wq_dma
                last_qkv_t = None
                for i in range(2):
                    for sec, base in ((0, 0), (1, 4096)):
                        for qt in range(4):
                            bank = PS[2 * sec + qt % 2]
                            bname = f"b{2 * sec + qt % 2}"
                            tvv = None
                            for c in range(8):
                                wg = pl.twrite(bname) if c == 0 else []
                                col = (c * 3 + sec) * 256 + 128 * i
                                tvv = pl.t(
                                    lambda e, bank=bank, c=c, col=col, qt=qt:
                                    e.matmul(bank[:, :], WQS[:, col:col+128],
                                             lnc(c, 512*qt, 512),
                                             start=(c == 0), stop=(c == 7)),
                                    wait=[("DG", wq_ready)] + wg,
                                    inc="T" if c == 7 else None)
                            sv = pl.s(
                                lambda e, bank=bank, base=base, i=i, qt=qt:
                                e.activation(
                                    W48[:, base + 2048*i + 512*qt:
                                           base + 2048*i + 512*qt + 512],
                                    bank[:, :], AF.Copy),
                                wait=[("T", tvv)], inc="S")
                            pl.consume(bname, "S", sv)
                            last_qkv_t = tvv
                for tt in range(16):
                    bank = PS[4 + tt % 2]
                    bname = f"b{4 + tt % 2}"
                    tvv = None
                    for c in range(8):
                        wg = pl.twrite(bname) if c == 0 else []
                        col = (c * 3 + 2) * 256
                        tvv = pl.t(
                            lambda e, bank=bank, c=c, col=col, tt=tt:
                            e.matmul(bank[:, 0:256], lnc(c, 128*tt, 128),
                                     WQS[:, col:col+256],
                                     start=(c == 0), stop=(c == 7)),
                            wait=[("DG", wq_ready)] + wg,
                            inc="T" if c == 7 else None)
                    vout = W48.ap()[:, 8192 + 384*tt: 8192 + 384*tt + 384] \
                        .rearrange("p (ip t3 r) -> p ip t3 r", ip=2, t3=3)[:, :, 0:3:2, :]
                    vv = pl.v(lambda e, bank=bank, vout=vout: e.tensor_copy(
                        vout,
                        bank[:, 0:256].rearrange("p (ip hl r) -> p ip hl r",
                                                 ip=2, hl=2)),
                        wait=[("T", tvv)], inc="V")
                    pl.consume(bname, "V", vv)
                    last_qkv_t = tvv
                if q < 3:
                    wq_dma = load_w(WQS[:, :], d_wqkvb[q + 1],
                                    wait=[("T", last_qkv_t)])
                pl.barrier()

                for i in range(2):
                    ach = 2 * q + i
                    kvevac = {}
                    for cc in range(16):
                        par = cc % 2
                        b0 = 2048 * i + 128 * cc
                        wg = pl.twrite("psb")
                        ttr = pl.t(lambda e, b0=b0, par=par: e.transpose(
                            PSB[:, 128*par: 128*par + 128],
                            W48[:, 4096 + b0: 4096 + b0 + 128], IDB[:, :]),
                            wait=wg, inc="T")
                        stk = pl.s(lambda e, par=par: e.copy(
                            KTOK[:, 128*par: 128*par + 128],
                            PSB[:, 128*par: 128*par + 128]),
                            wait=[("T", ttr)], inc="S")
                        pl.consume("psb", "S", stk)
                        smm = {}
                        for hl in range(2):
                            bank = PS[2 * hl + par]
                            rows = slice(64 * hl, 64 * hl + 64)
                            wg = pl.twrite(f"b{2*hl+par}")
                            smm[hl] = pl.t(lambda e, bank=bank, rows=rows, b0=b0,
                                                  hl=hl:
                                           e.matmul(bank[:, 0:128],
                                                    W48[rows, 4096 + b0:
                                                              4096 + b0 + 128],
                                                    W48[rows, b0: b0 + 128],
                                                    start=True, stop=True,
                                                    tile_position=(64 * hl, 0)),
                                           wait=wg, inc="T")
                        pmv = {}
                        for hl in range(2):
                            bank = PS[2 * hl + par]
                            ptb = PTB[:, 256*par + 128*hl: 256*par + 128*hl + 128]
                            pmv[hl] = pl.v(lambda e, bank=bank, ptb=ptb:
                                           e.tensor_tensor(ptb, bank[:, 0:128],
                                                           TRIU[:, :], ALU.mult),
                                           wait=[("T", smm[hl])], inc="V")
                            pl.consume(f"b{2*hl+par}", "V", pmv[hl])
                        ybank = PS[4 + par]
                        wg = pl.twrite(f"b{4 + par}")
                        for hl in range(2):
                            ptb = PTB[:, 256*par + 128*hl: 256*par + 128*hl + 128]
                            vcol = 8192 + 384*cc + 192*i + 64*hl
                            pl.t(lambda e, ybank=ybank, ptb=ptb, vcol=vcol, hl=hl:
                                 e.matmul(ybank[:, 0:128],
                                          W48[:, vcol: vcol + 128],
                                          ptb, start=(hl == 0),
                                          stop=(cc == 0 and hl == 1)),
                                 wait=[("V", pmv[hl])] + (wg if hl == 0 else []))
                            if cc > 0:
                                pl.t(lambda e, ybank=ybank, par=par, hl=hl, b0=b0:
                                     e.matmul(ybank[:, 0:128],
                                              KVSB[:, 192*par + 64*hl:
                                                      192*par + 64*hl + 128],
                                              W48[:, b0: b0 + 128],
                                              start=False, stop=(hl == 1)),
                                     wait=[("S", kvevac.get(cc - 1, 0))])
                        tlast = pl.emit("t", None, (), "T")
                        wg = pl.twrite("b6")
                        tku = None
                        for hl in range(2):
                            vcol = 8192 + 384*cc + 192*i + 128*hl
                            tku = pl.t(lambda e, par=par, hl=hl, cc=cc, vcol=vcol:
                                       e.matmul(PS[6][64*hl:64*hl+64, 0:64],
                                                KTOK[:, 128*par + 64*hl:
                                                        128*par + 64*hl + 64],
                                                W48[:, vcol: vcol + 64],
                                                start=(cc == 0),
                                                stop=True,
                                                tile_position=(0, 64 * hl)),
                                       wait=[("S", stk)] + (wg if hl == 0 else []),
                                       inc="T")
                        if cc < 15:
                            par2 = (cc + 1) % 2
                            pl.s(lambda e, par2=par2: e.copy(
                                KVSB[0:64, 192*par2: 192*par2 + 64],
                                PS[6][0:64, 0:64]), wait=[("T", tku)], inc="S")
                            sv2 = pl.s(lambda e, par2=par2: e.copy(
                                KVSB[64:128, 192*par2 + 128: 192*par2 + 192],
                                PS[6][64:128, 0:64]), inc="S")
                            kvevac[cc] = sv2
                            pl.consume("b6", "S", sv2)
                        else:
                            pl.consume("b6", "T", tku)
                        syv = pl.s(lambda e, ybank=ybank, ach=ach, cc=cc: e.copy(
                            ARN[:, 2048*ach + 128*cc: 2048*ach + 128*cc + 128],
                            ybank[:, 0:128]), wait=[("T", tlast)], inc="S")
                        pl.consume(f"b{4 + par}", "S", syv)
                pl.barrier()

            dump_arn("ynb")
            dump_w48b("wcp")
            for m in range(8):
                for qt in range(4):
                    bank = PS[(4 * m + qt) % 2]
                    bname = f"b{(4 * m + qt) % 2}"
                    tvv = None
                    for c in range(8):
                        wg = pl.twrite(bname) if c == 0 else []
                        tvv = pl.t(lambda e, bank=bank, c=c, m=m, qt=qt:
                                   e.matmul(bank[:, :],
                                            W48[:, 16384 + 1024*c + 128*m:
                                                   16384 + 1024*c + 128*m + 128],
                                            ARN[:, 2048*c + 512*qt:
                                                   2048*c + 512*qt + 512],
                                            start=(c == 0), stop=(c == 7)),
                                   wait=wg, inc="T" if c == 7 else None)
                    vv = pl.v(lambda e, bank=bank, m=m, qt=qt: e.tensor_tensor(
                        xc(m, 512*qt, 512), xc(m, 512*qt, 512), bank[:, :],
                        ALU.add), wait=[("T", tvv)], inc="V")
                    pl.consume(bname, "V", vv)
            pl.barrier()

        def dump_x(nm):
            if nm not in d_dump:
                return
            pl.barrier()
            for c in range(8):
                pl.y(lambda e, nm=nm, c=c: e.dma_start(
                    out=d_dump[nm][:, C2*c:C2*(c+1)], in_=X[:, C2*c:C2*(c+1)]),
                    inc="DO", inc_n=16, dma=True)
            pl.barrier()

        def dump_ln(nm):
            if nm not in d_dump:
                return
            pl.barrier()
            pl.g(lambda e, nm=nm: e.dma_start(out=d_dump[nm], in_=LNB[:, :]),
                 inc="DO", inc_n=16, dma=True)
            pl.barrier()

        def dump_arn(nm):
            if nm not in d_dump:
                return
            pl.barrier()
            pl.g(lambda e, nm=nm: e.dma_start(out=d_dump[nm], in_=ARN[:, :]),
                 inc="DO", inc_n=16, dma=True)
            pl.barrier()

        def dump_mst(nm):
            if nm not in d_dump:
                return
            pl.barrier()
            pl.g(lambda e, nm=nm: e.dma_start(out=d_dump[nm][:, 0:4096],
                                              in_=MST[:, :]),
                 inc="DO", inc_n=16, dma=True)
            pl.g(lambda e, nm=nm: e.dma_start(out=d_dump[nm][:, 4096:6144],
                                              in_=PTL[:, :]),
                 inc="DO", inc_n=16, dma=True)
            pl.barrier()

        def dump_w48b(nm):
            if nm not in d_dump:
                return
            pl.barrier()
            pl.g(lambda e, nm=nm: e.dma_start(out=d_dump[nm][:, 0:12288],
                                              in_=W48[:, 12288:24576]),
                 inc="DO", inc_n=16, dma=True)
            pl.barrier()

        def dump_w48(nm):
            if nm not in d_dump:
                return
            pl.barrier()
            pl.g(lambda e, nm=nm: e.dma_start(out=d_dump[nm][:, 0:12288],
                                              in_=W48[:, 0:12288]),
                 inc="DO", inc_n=16, dma=True)
            pl.barrier()

        if "ln1" in stages and "attn_a" not in stages:
            ph_layernorm()
            dump_ln("ln1")
        if "attn_a" in stages:
            ph_layernorm()
            dump_ln("ln1")
            pl.barrier()
            ph_attn_a()
            dump_x("x1")
        if "ffn1" in stages:
            ph_layernorm()
            dump_ln("ln2")
            pl.barrier()
            ph_ffn()
            dump_x("x2")
        if "attn_b" in stages:
            ph_layernorm()
            dump_ln("ln3")
            pl.barrier()
            ph_attn_b()
            dump_x("x3")
        if "ffn2" in stages:
            ph_layernorm()
            dump_ln("ln4")
            pl.barrier()
            ph_ffn()

        pl.barrier()
        for c in range(8):
            pl.y(lambda e, c=c: e.dma_start(out=d_out[128*c:128*(c+1), :],
                                            in_=X[:, C2*c:C2*(c+1)]),
                 inc="DO", inc_n=16, dma=True)
        do_final = pl.cnt["DO"]
        pl.emit("y", None, [("DO", do_final)])

        pl.finalize()
        with nc.Block() as block:
            def run_queue(key):
                def body(e):
                    for ent in pl.q[key]:
                        for k, val in ent.waits:
                            e.wait_ge(sem[k], val)
                        if ent.thunk is not None:
                            ins = ent.thunk(e)
                            if ent.inc is not None:
                                ins.then_inc(sem[ent.inc], ent.inc_n)
                        elif ent.mode == "seminc":
                            if ent.prewait is not None:
                                e.wait_ge(sem[ent.prewait[0]], ent.prewait[1])
                            e.sem_inc(sem[ent.inc], ent.inc_n)
                return body
            block.tensor(run_queue("t"))
            block.vector(run_queue("v"))
            block.scalar(run_queue("s"))
            block.gpsimd(run_queue("g"))
            block.sync(run_queue("y"))

    return nc


def _pack_inputs(L0_attn_a_x, L0_attn_a_W_qkv, L0_attn_a_b_qkv, L0_attn_a_W_proj,
                 L0_attn_a_b_proj, L0_attn_b_Wq, L0_attn_b_Wk, L0_attn_b_Wv,
                 L0_attn_b_Wc, L0_attn_b_Ltri, L0_ffn_W_r, L0_ffn_W_k, L0_ffn_W_v,
                 L0_ffn_mu_r, L0_ffn_mu_k):
    asf = lambda a: np.ascontiguousarray(np.asarray(a, dtype=np.float32))
    x = asf(L0_attn_a_x)
    W_qkv = asf(L0_attn_a_W_qkv)
    b_qkv = asf(L0_attn_a_b_qkv)
    W_proj = asf(L0_attn_a_W_proj)
    b_proj = asf(L0_attn_a_b_proj)

    def pack_qkv(w):
        a = w.reshape(8, 128, 3, 4, 256)
        return np.ascontiguousarray(a.transpose(3, 1, 0, 2, 4).reshape(4, 128, 6144))

    def pack_cm(wT):
        return np.ascontiguousarray(
            wT.reshape(8, 128, 1024).transpose(1, 0, 2).reshape(128, 8192))

    wqkv = pack_qkv(W_qkv)
    cat_b = np.ascontiguousarray(np.concatenate(
        [asf(L0_attn_b_Wq).T, asf(L0_attn_b_Wk).T, asf(L0_attn_b_Wv).T], axis=1))
    wqkvb = pack_qkv(cat_b)
    wproj = np.ascontiguousarray(
        W_proj.reshape(8, 128, 2, 512).transpose(2, 1, 0, 3).reshape(2, 128, 4096))
    wc = pack_cm(np.ascontiguousarray(asf(L0_attn_b_Wc).T))
    wr = pack_cm(np.ascontiguousarray(asf(L0_ffn_W_r).T))
    wk2 = pack_cm(np.ascontiguousarray(asf(L0_ffn_W_k).T))
    wv2 = pack_cm(np.ascontiguousarray(asf(L0_ffn_W_v).T))
    ltriT = np.ascontiguousarray(
        np.transpose(asf(L0_attn_b_Ltri), (2, 0, 1)).reshape(64, 1024))
    vecs = np.zeros((128, 48), np.float32)
    vecs[:, 0:24] = b_qkv.reshape(24, 128).T
    vecs[:, 24:32] = b_proj.reshape(8, 128).T
    vecs[:, 32:40] = asf(L0_ffn_mu_r).reshape(8, 128).T
    vecs[:, 40:48] = asf(L0_ffn_mu_k).reshape(8, 128).T
    bvrow = np.ascontiguousarray(b_qkv[2048:3072].reshape(1, 1024))
    ident = np.eye(128, dtype=np.float32)
    triu = (np.arange(128)[:, None] <= np.arange(128)[None, :]).astype(np.float32)
    iotaf = (np.arange(128)[:, None] - np.arange(2048)[None, :]).astype(np.float16)
    onbs = np.zeros((128, 256), np.float32)
    onbs[:, 0] = 1.0
    onbs[:, 128 + 32] = 1.0

    shared = dict(wqkv=wqkv, wqkvb=wqkvb, wproj=wproj, wc=wc, wr=wr,
                  wk2=wk2, wv2=wv2, ltriT=ltriT, vecs=vecs, bvrow=bvrow,
                  ident=ident, triu=triu, iotaf=iotaf, onbs=onbs)
    in_maps = []
    for b in range(B):
        m = dict(shared)
        m["xT"] = np.ascontiguousarray(x[b].T)
        in_maps.append(m)
    return in_maps


def kernel(**inputs):
    in_maps = _pack_inputs(**inputs)
    nc = build_program()
    res = run_bass_kernel_spmd(nc, in_maps, core_ids=list(range(B)))
    out = np.stack([np.asarray(r["out"]).T for r in res.results], axis=0)
    return np.ascontiguousarray(out.astype(np.float32))
